# version 06
# speedup vs baseline: 1.1040x; 1.0210x over previous
import math
import numpy as np

import concourse.bass as bass
import concourse.mybir as mybir
from concourse.bass_utils import run_bass_kernel_spmd

F32 = mybir.dt.float32
BF16 = mybir.dt.bfloat16
AF = mybir.ActivationFunctionType
ALU = mybir.AluOpType
AX = mybir.AxisListType.X

T = 2048
D = 1024
NT = T // 128
ND = D // 128

ENGINES = ("tensor", "vector", "scalar", "gpsimd", "sync")
DMA_CHANNELS = ("dma_w", "dma_x", "dma_o")
DMA_LANES = {"dma_w": 6, "dma_x": 3, "dma_o": 2}

SLAB_BASE = 0x4500
SLAB_BYTES = 0x33800



class MK:
    def __init__(self, nc):
        self.nc = nc
        self.ops = {e: [] for e in ENGINES}
        self.cnt = {s: 0 for s in ENGINES + DMA_CHANNELS}
        self.regions = {}
        self.vclock = {e: {} for e in ENGINES}
        self.snap = {}
        self.wait_targets = {s: set() for s in ENGINES + DMA_CHANNELS}
        self.sems = {}
        self.n_waits = 0

    def _need(self, eng, stream, idx):
        if stream == eng and eng == "tensor":
            return
        if self.vclock[eng].get(stream, -1) >= idx:
            return
        self.wait_targets[stream].add(idx)
        self._pend.append((stream, idx))
        vc = self.vclock[eng]
        for s2, i2 in self.snap[(stream, idx)].items():
            if vc.get(s2, -1) < i2:
                vc[s2] = i2
        self.n_waits += 1

    def _resolve(self, eng, reads, writes):
        self._pend = []
        for key in reads:
            reg = self.regions.get(key)
            if reg and reg["w"] is not None:
                self._need(eng, *reg["w"])
        for key in writes:
            reg = self.regions.get(key)
            if reg:
                if reg["w"] is not None:
                    self._need(eng, *reg["w"])
                for (s, i) in reg["r"]:
                    self._need(eng, s, i)
        return self._pend

    def _record(self, stream, idx, reads, writes):
        for key in reads:
            reg = self.regions.setdefault(key, {"w": None, "r": []})
            reg["r"].append((stream, idx))
        for key in writes:
            reg = self.regions.setdefault(key, {"w": None, "r": []})
            reg["w"] = (stream, idx)
            reg["r"] = []

    def op(self, eng, emit, r=(), w=()):
        waits = self._resolve(eng, r, w)
        idx = self.cnt[eng]
        self.cnt[eng] += 1
        clock = dict(self.vclock[eng])
        clock[eng] = idx
        self.snap[(eng, idx)] = clock
        self._record(eng, idx, r, w)
        self.ops[eng].append((waits, emit, eng, idx))
        return (eng, idx)

    def dma(self, eng, ch, emit, r=(), w=()):
        waits = self._resolve(eng, r, w)
        idx = self.cnt[eng]
        self.cnt[eng] += 1
        cidx = self.cnt[ch]
        self.cnt[ch] += 1
        clock = dict(self.vclock[eng])
        clock[eng] = idx
        self.snap[(eng, idx)] = clock
        cclock = dict(clock)
        cclock[ch] = cidx
        self.snap[(ch, cidx)] = cclock
        self._record(ch, cidx, r, w)
        self.ops[eng].append((waits, emit, ch, cidx))
        return (ch, cidx)

    def barrier_read(self, eng, keys):
        waits = self._resolve(eng, keys, ())
        if waits:
            idx = self.cnt[eng]
            self.cnt[eng] += 1
            clock = dict(self.vclock[eng])
            clock[eng] = idx
            self.snap[(eng, idx)] = clock
            self.ops[eng].append((waits, None, eng, idx))

    def finalize(self, block):
        ranks = {}
        for s in ENGINES:
            flagged = sorted(self.wait_targets[s])
            ranks[s] = {idx: r + 1 for r, idx in enumerate(flagged)}
        self._sem_guards = []
        for s in ENGINES:
            g = self.nc.semaphore("sem_" + s)
            self.sems[s] = g.__enter__()
            self._sem_guards.append(g)
        for s in DMA_CHANNELS:
            for ln in range(DMA_LANES[s]):
                g = self.nc.semaphore(f"sem_{s}_{ln}")
                self.sems[(s, ln)] = g.__enter__()
                self._sem_guards.append(g)

        def wait_sem_val(stream, idx):
            if stream in DMA_CHANNELS:
                L = DMA_LANES[stream]
                return self.sems[(stream, idx % L)], 16 * (idx // L + 1)
            fl = ranks[stream]
            if idx in fl:
                v = fl[idx]
            else:
                later = [v2 for k, v2 in fl.items() if k >= idx]
                assert later, f"wait on unflagged {stream}:{idx}"
                v = min(later)
            return self.sems[stream], v

        mkself = self

        def section_for(ename):
            def section(eng):
                for waits, emit, inc_stream, inc_idx in mkself.ops[ename]:
                    for (ws, wi) in waits:
                        sem, val = wait_sem_val(ws, wi)
                        eng.wait_ge(sem, val)
                    if emit is None:
                        continue
                    ins = emit(eng)
                    if inc_stream in DMA_CHANNELS:
                        L = DMA_LANES[inc_stream]
                        ins.then_inc(mkself.sems[(inc_stream, inc_idx % L)], 16)
                    elif inc_idx in ranks[inc_stream]:
                        ins.then_inc(mkself.sems[inc_stream], 1)
            return section

        block.tensor(section_for("tensor"))
        block.vector(section_for("vector"))
        block.scalar(section_for("scalar"))
        block.gpsimd(section_for("gpsimd"))
        block.sync(section_for("sync"))



class Slab:
    def __init__(self, nc, mk):
        self.nc = nc
        self.mk = mk
        self.free_list = [(0, SLAB_BYTES)]
        self.live = {}
        self.freed = []
        self.peak = 0

    def alloc(self, name, pdim, cols, dtype, keys=None):
        assert name not in self.live, name
        esz = 2 if dtype == BF16 else 4
        size = (cols * esz + 63) & ~63
        small = size < 8192
        order = (range(len(self.free_list) - 1, -1, -1) if small
                 else range(len(self.free_list)))
        for i in order:
            off, sz = self.free_list[i]
            if sz >= size:
                if sz == size:
                    self.free_list.pop(i)
                elif small:
                    off = off + sz - size
                    self.free_list[i] = (self.free_list[i][0], sz - size)
                else:
                    self.free_list[i] = (off + size, sz - size)
                h = self.nc.alloc_sbuf_tensor_at(
                    f"sl_{name}", [pdim, cols], dtype, offset=SLAB_BASE + off)
                keys = list(keys) if keys else [name]
                self.live[name] = (off, size, h, keys)
                deps = []
                nf = []
                for (fo, fe, dents) in self.freed:
                    if fo < off + size and off < fe:
                        deps.extend(dents)
                        if fo < off:
                            nf.append((fo, off, dents))
                        if fe > off + size:
                            nf.append((off + size, fe, dents))
                    else:
                        nf.append((fo, fe, dents))
                self.freed = nf
                if deps:
                    for k in keys:
                        reg = self.mk.regions.setdefault(k, {"w": None, "r": []})
                        reg["r"] = list(set(reg["r"] + deps))
                self.peak = max(self.peak, SLAB_BYTES - sum(s for _, s in self.free_list))
                return h
        raise MemoryError(f"slab alloc failed for {name} ({size}B) "
                          f"free={self.free_list}")

    def free(self, name):
        off, size, _, keys = self.live.pop(name)
        dents = []
        for k in keys:
            reg = self.mk.regions.get(k)
            if reg:
                if reg["w"] is not None:
                    dents.append(reg["w"])
                dents.extend(reg["r"])
        self.freed.append((off, off + size, list(set(dents))))
        self.free_list.append((off, size))
        self.free_list.sort()
        merged = []
        for o, sz in self.free_list:
            if merged and merged[-1][0] + merged[-1][1] == o:
                merged[-1] = (merged[-1][0], merged[-1][1] + sz)
            else:
                merged.append((o, sz))
        self.free_list = [tuple(x) for x in merged]



def _fmpack(w):
    K, N = w.shape
    assert K % 128 == 0
    return np.ascontiguousarray(
        w.reshape(K // 128, 128, N).transpose(1, 0, 2).reshape(128, (K // 128) * N))


def _colpack(v):
    C = v.shape[0]
    return np.ascontiguousarray(v.reshape(C // 128, 128).T)


def prep_host(inputs):
    g = {}
    x = np.asarray(inputs["L0_attn_a_x"])
    xs = [np.ascontiguousarray(x[i]) for i in range(8)]

    g["wq"] = _fmpack(np.asarray(inputs["L0_attn_a_Wq"]))
    g["wk"] = _fmpack(np.asarray(inputs["L0_attn_a_Wk"]))
    g["wv"] = _fmpack(np.asarray(inputs["L0_attn_a_Wv"]))
    g["wo"] = _fmpack(np.asarray(inputs["L0_attn_a_Wo"]))

    g["sim"] = _fmpack(np.asarray(inputs["L0_ffn_sim_matrix"]))
    g["gates"] = np.ascontiguousarray(np.asarray(inputs["L0_ffn_gates"]).reshape(1, 16))
    qp = np.asarray(inputs["L0_ffn_q_proj"])
    kp = np.asarray(inputs["L0_ffn_k_proj"])
    vp = np.asarray(inputs["L0_ffn_v_proj"])
    opj = np.asarray(inputs["L0_ffn_o_proj"])
    g["qall"] = _fmpack(np.ascontiguousarray(qp.transpose(1, 0, 2).reshape(1024, 1024)))
    g["kall"] = _fmpack(np.ascontiguousarray(kp.transpose(1, 0, 2).reshape(1024, 1024)))
    g["vall"] = _fmpack(np.ascontiguousarray(vp.transpose(1, 0, 2).reshape(1024, 1024)))
    g["oall"] = _fmpack(np.ascontiguousarray(opj.reshape(1024, 1024)))
    S = np.tile(np.eye(64, dtype=np.float32), (16, 1))
    g["spat"] = _fmpack(S)
    g["spatT"] = np.ascontiguousarray(S.T)
    g["ppat"] = np.ascontiguousarray(
        np.repeat(np.eye(16, dtype=np.float32), 64, axis=1))

    g["win"] = _fmpack(np.asarray(inputs["L0_attn_b_W_in"]))
    g["convw"] = np.ascontiguousarray(
        np.asarray(inputs["L0_attn_b_conv_w"]).reshape(18, 128, 4)
        .transpose(1, 0, 2).reshape(128, 72))
    g["convb"] = _colpack(np.asarray(inputs["L0_attn_b_conv_b"]))
    g["alog"] = np.ascontiguousarray(np.asarray(inputs["L0_attn_b_A_log"]).reshape(32, 1))
    g["dm"] = np.ascontiguousarray(np.asarray(inputs["L0_attn_b_Dm"]).reshape(1, 32))
    g["dtb"] = np.ascontiguousarray(np.asarray(inputs["L0_attn_b_dt_bias"]).reshape(32, 1))
    g["mnorm"] = _colpack(np.asarray(inputs["L0_attn_b_mnorm_w"]))
    g["wout"] = _fmpack(np.asarray(inputs["L0_attn_b_W_out"]))
    g["wqkv"] = _fmpack(np.asarray(inputs["L0_attn_b_W_qkv"]))
    g["wcproj"] = _fmpack(np.asarray(inputs["L0_attn_b_W_cproj"]))
    g["qconvw"] = np.ascontiguousarray(
        np.tile(np.asarray(inputs["L0_attn_b_qconv_w"]), (16, 1))
        .reshape(8, 128, 3).transpose(1, 0, 2).reshape(128, 24))
    g["qconvb"] = _colpack(np.tile(np.asarray(inputs["L0_attn_b_qconv_b"]), 16))
    kw = np.asarray(inputs["L0_attn_b_kconv_w"])
    kb = np.asarray(inputs["L0_attn_b_kconv_b"]).reshape(64, 1)
    vw = np.asarray(inputs["L0_attn_b_vconv_w"])
    vb = np.asarray(inputs["L0_attn_b_vconv_b"]).reshape(64, 1)
    g["kvconv"] = np.ascontiguousarray(
        np.concatenate([np.concatenate([kw, kb], 1),
                        np.concatenate([vw, vb], 1)], 0))
    g["kkconv"] = np.ascontiguousarray(
        np.tile(np.concatenate([kw, kb], 1), (2, 1)))
    g["maak"] = _colpack(np.asarray(inputs["L0_attn_b_time_maa_k"]))
    g["maar"] = _colpack(np.asarray(inputs["L0_attn_b_time_maa_r"]))
    g["wkey"] = _fmpack(np.asarray(inputs["L0_attn_b_W_key"]))
    g["wrec"] = _fmpack(np.asarray(inputs["L0_attn_b_W_rec"]))
    g["wval"] = _fmpack(np.asarray(inputs["L0_attn_b_W_val"]))
    return g, xs



class Prog:
    def __init__(self, stop_after=None):
        self.stop_after = stop_after
        nc = bass.Bass("TRN2", target_bir_lowering=False, debug=False)
        self.nc = nc
        self.mk = MK(nc)
        self.slab = Slab(nc, self.mk)
        self.dram = {}
        self.psf = [nc.alloc_psum_tensor(f"psf{i}", [128, 512], F32)
                    for i in range(6)]
        self.psb = [nc.alloc_psum_tensor(f"psb{i}", [128, 1024], BF16)
                    for i in range(2)]

    def din(self, name, shape):
        self.dram[name] = self.nc.dram_tensor(name, list(shape), F32,
                                              kind="ExternalInput").ap()

    def dout(self, name, shape):
        self.dram[name] = self.nc.dram_tensor(name, list(shape), F32,
                                              kind="ExternalOutput").ap()

    def dscratch(self, name, shape, dtype):
        self.dram[name] = self.nc.dram_tensor(name, list(shape), dtype,
                                              kind="Internal").ap()


def tk(buf, i):
    return f"{buf}:{i}"


def AP_IDENT4(P):
    from concourse.ap import AP as _AP
    a = P.ident_bf[:]
    return _AP(a.tensor, a.offset, [a.ap[0], [0, 4]] + [a.ap[1]])


def AP_COLREP(colsap, reps):
    from concourse.ap import AP as _AP
    return _AP(colsap.tensor, colsap.offset,
               [colsap.ap[0], colsap.ap[1], [0, reps]])


class StopBuild(Exception):
    pass


def dump(P, tag, ap, rkeys):
    if getattr(P, "dbg_tag", None) != tag:
        return
    sh = ap.shape
    mkd = P.mk
    if len(sh) == 3:
        dst = P.dram["y"].rearrange("(i p) f -> p i f", p=128)[:, 0:sh[1], 0:sh[2]]
    else:
        dst = P.dram["y"].rearrange("a b -> (a b)")[0:sh[0] * sh[1]].rearrange(
            "(p f) -> p f", p=sh[0])
    mkd.dma("gpsimd", "dma_w", lambda e: e.dma_start(out=dst, in_=ap),
            r=rkeys, w=["y_out"])
    mkd.barrier_read("gpsimd", ["y_out"])
    raise StopBuild


def build_constants(P):
    mk, slab = P.mk, P.slab
    P.ident_bf = slab.alloc("ident_bf", 128, 128, BF16)
    P.ident_f = slab.alloc("ident_f", 128, 128, F32)
    P.iota_sc = slab.alloc("iota_sc", 128, 129, F32)
    P.ones_row_b = slab.alloc("ones_row_b", 1, 128, BF16)
    P.ones_row_f = slab.alloc("ones_row_f", 1, 128, F32)
    P.ones_all_f = slab.alloc("ones_all_f", 128, 128, F32)
    P.ones_all_b = slab.alloc("ones_all_b", 128, 128, BF16)
    mk.op("gpsimd", lambda e: e.iota(P.iota_sc[:, 0:1], [[0, 1]],
                                     channel_multiplier=1,
                                     allow_small_or_imprecise_dtypes=True),
          w=["iota_sc"])
    mk.op("gpsimd", lambda e: e.iota(P.iota_sc[:, 1:129], [[1, 128]],
                                     channel_multiplier=0,
                                     allow_small_or_imprecise_dtypes=True),
          r=["iota_sc"], w=["iota_sc"])
    mk.op("vector", lambda e: e.tensor_scalar(P.ident_f[:], P.iota_sc[:, 1:129],
                                              P.iota_sc[:, 0:1], None,
                                              ALU.is_equal),
          r=["iota_sc"], w=["ident_f"])
    mk.op("vector", lambda e: e.tensor_copy(P.ident_bf[:], P.ident_f[:]),
          r=["ident_f"], w=["ident_bf"])
    mk.op("vector", lambda e: e.memset(P.ones_row_b[:], 1.0), w=["ones_row_b"])
    mk.op("vector", lambda e: e.memset(P.ones_row_f[:], 1.0), w=["ones_row_f"])
    mk.op("vector", lambda e: e.memset(P.ones_all_f[:], 1.0), w=["ones_all_f"])
    mk.op("vector", lambda e: e.memset(P.ones_all_b[:], 1.0), w=["ones_all_b"])
    P.cc = P.slab.alloc("cc", 128, 4, F32)
    for j, val in ((0, 1e-5), (1, 1e-6), (2, 1.0), (3, 0.0)):
        mk.op("vector", lambda e, j=j, val=val: e.memset(P.cc[:, j:j + 1], val),
              r=["cc"], w=["cc"])


def load_weight_bf16(P, name, cols, key=None, subkeys=None):
    key = key or name
    h = P.slab.alloc(key, 128, cols, BF16,
                     keys=subkeys if subkeys else [key])
    wr = subkeys if subkeys else [key]
    P.mk.dma("gpsimd", "dma_w",
             lambda e, h=h, name=name: e.dma_start(out=h[:], in_=P.dram[name]),
             w=wr)
    return h


def load_x(P):
    P.xbuf = P.slab.alloc("x", 128, NT * D, F32,
                          keys=[tk("x", i) for i in range(NT)])
    src = P.dram["x"].rearrange("(i p) f -> p i f", p=128)
    dst = P.xbuf[:].rearrange("p (i f) -> p i f", i=NT)
    P.mk.dma("sync", "dma_x", lambda e: e.dma_start(out=dst, in_=src),
             w=[tk("x", i) for i in range(NT)])


def store_y(P, buf, bufname):
    dst = P.dram["y"].rearrange("(i p) f -> p i f", p=128)
    src = buf[:].rearrange("p (i f) -> p i f", i=NT)
    P.mk.dma("sync", "dma_o",
             lambda e: e.dma_start(out=dst, in_=src),
             r=[tk(bufname, i) for i in range(NT)], w=["y_out"])
    P.mk.barrier_read("sync", ["y_out"])



def transpose_tile_to_fm(P, srckey, src_ap_fn, fm, fmname, i, nchunks=ND,
                         fmT=T):
    mk = P.mk
    for g in range(0, nchunks, 4):
        n = min(4, nchunks - g)
        pb = P.psb[(g // 4) % 2]
        pkey = f"psb{(g // 4) % 2}"
        for k in range(n):
            c = g + k
            mk.op("tensor", lambda e, c=c, k=k, pb=pb: e.transpose(
                pb[:, k * 128:(k + 1) * 128], src_ap_fn(c), P.ident_bf[:]),
                r=[srckey, "ident_bf"], w=[pkey])
        dst = fm[:].rearrange("p (c t) -> p c t", t=fmT)[:, g:g + n,
                                                         i * 128:(i + 1) * 128]
        srcv = pb[:, 0:n * 128].rearrange("p (c t) -> p c t", c=n)
        mk.op("vector", lambda e, dst=dst, srcv=srcv: e.tensor_copy(dst, srcv),
              r=[pkey], w=[f"{fmname}:{c}" for c in range(g, g + n)])


def layernorm_fm(P, xkey, xbuf, fmname, eps=1e-5, add_back=True, rms=False,
                 keep_cols=False):
    mk, slab = P.mk, P.slab
    cols = slab.alloc(fmname + "_c", 128, 96, F32, keys=[fmname + "_c"])
    fm = slab.alloc(fmname, 128, ND * T, BF16,
                    keys=[f"{fmname}:{c}" for c in range(ND)])
    sc = [slab.alloc(fmname + f"_sc{p}", 128, D, BF16) for p in range(2)]
    ck = fmname + "_c"

    for i in range(NT):
        xt = xbuf[:, i * D:(i + 1) * D]
        s = sc[i % 2]
        skey = fmname + f"_sc{i % 2}"
        mk.op("scalar", lambda e, xt=xt, i=i, s=s: e.activation(
            s[:], xt, AF.Square, accum_out=cols[:, 16 + i:17 + i]),
            r=[tk(xkey, i)], w=[ck, skey])
        if not rms:
            mk.op("vector", lambda e, xt=xt, i=i: e.reduce_sum(
                cols[:, 0 + i:1 + i], xt, axis=AX),
                r=[tk(xkey, i)], w=[ck])
    if rms:
        mk.op("vector", lambda e: e.tensor_scalar(
            cols[:, 32:48], cols[:, 16:32], 1.0 / D, eps, ALU.mult, ALU.add),
            r=[ck], w=[ck])
        mk.op("scalar", lambda e: e.activation(cols[:, 32:48], cols[:, 32:48],
                                               AF.Sqrt),
              r=[ck], w=[ck])
        mk.op("vector", lambda e: e.reciprocal(cols[:, 48:64], cols[:, 32:48]),
              r=[ck], w=[ck])
        mk.op("vector", lambda e: e.memset(cols[:, 64:80], 0.0), r=[ck], w=[ck])
    else:
        mk.op("vector", lambda e: e.tensor_scalar_mul(cols[:, 0:16],
                                                      cols[:, 0:16], 1.0 / D),
              r=[ck], w=[ck])
        mk.op("vector", lambda e: e.tensor_scalar(
            cols[:, 32:48], cols[:, 16:32], 1.0 / D, eps, ALU.mult, ALU.add),
            r=[ck], w=[ck])
        mk.op("vector", lambda e: e.scalar_tensor_tensor(
            cols[:, 64:80], cols[:, 0:16], -1.0, cols[:, 0:16],
            ALU.mult, ALU.mult),
            r=[ck], w=[ck])
        mk.op("vector", lambda e: e.tensor_add(cols[:, 32:48], cols[:, 32:48],
                                               cols[:, 64:80]),
              r=[ck], w=[ck])
        mk.op("scalar", lambda e: e.activation(cols[:, 32:48], cols[:, 32:48],
                                               AF.Sqrt),
              r=[ck], w=[ck])
        mk.op("vector", lambda e: e.reciprocal(cols[:, 48:64], cols[:, 32:48]),
              r=[ck], w=[ck])
        mk.op("vector", lambda e: e.scalar_tensor_tensor(
            cols[:, 64:80], cols[:, 0:16], -1.0, cols[:, 48:64],
            ALU.mult, ALU.mult),
            r=[ck], w=[ck])
    if add_back:
        mk.op("vector", lambda e: e.tensor_scalar_add(cols[:, 32:48],
                                                      cols[:, 48:64], 1.0),
              r=[ck], w=[ck])

    for i in range(NT):
        xt = xbuf[:, i * D:(i + 1) * D]
        rstd = cols[:, 48 + i:49 + i]
        negm = cols[:, 64 + i:65 + i]
        sp1 = cols[:, 32 + i:33 + i]
        s = sc[i % 2]
        skey = fmname + f"_sc{i % 2}"
        mk.op("vector", lambda e, xt=xt, rstd=rstd, negm=negm, s=s: e.tensor_scalar(
            s[:], xt, rstd, negm, ALU.mult, ALU.add),
            r=[tk(xkey, i), ck], w=[skey])
        if add_back:
            mk.op("vector", lambda e, xt=xt, sp1=sp1, negm=negm: e.tensor_scalar(
                xt, xt, sp1, negm, ALU.mult, ALU.add),
                r=[ck, tk(xkey, i)], w=[tk(xkey, i)])
        transpose_tile_to_fm(P, skey, lambda c, s=s: s[:, c * 128:(c + 1) * 128],
                             fm, fmname, i)

    if getattr(P, "dbg_keep_cols", False):
        P.dbg_cols = cols
    elif not keep_cols:
        slab.free(fmname + "_c")
    slab.free(fmname + "_sc0")
    slab.free(fmname + "_sc1")
    if keep_cols:
        return fm, cols
    return fm



def matmul_fm_out(P, lhs_fn, rhs_fn, nk, out_dst, out_keys, rkeys, pbank,
                  evac=None, n=512):
    mk = P.mk
    ps = P.psf[pbank]
    pkey = f"psf{pbank}"
    for k in range(nk):
        mk.op("tensor", lambda e, k=k: e.matmul(ps[:, 0:n], lhs_fn(k), rhs_fn(k),
                                                start=(k == 0), stop=(k == nk - 1)),
              r=rkeys, w=[pkey])
    if evac is None:
        mk.op("vector", lambda e: e.tensor_copy(out_dst, ps[:, 0:n]),
              r=[pkey], w=out_keys)
    else:
        evac(ps, pkey)


def phase_attn_a(P, lnfm):
    mk, slab = P.mk, P.slab
    v65 = slab.alloc("v65a", 128, NT * 1040, BF16,
                     keys=[tk("v65a", t) for t in range(NT)])
    kfm = slab.alloc("kfm_a", 128, ND * T, BF16,
                     keys=[f"kfm_a:{c}" for c in range(ND)])
    qfm = slab.alloc("qfm_a", 128, ND * T, BF16,
                     keys=[f"qfm_a:{c}" for c in range(ND)])

    ones_v = v65[:].rearrange("p (t h c) -> p t h c", t=NT, h=16)[:, :, :, 64:65]
    mk.op("vector", lambda e: e.memset(ones_v, 1.0),
          w=[tk("v65a", t) for t in range(NT)])
    for fh in range(2):
        hk = f"wv_a_h{fh}"
        wvh = slab.alloc(hk, 128, ND * 512, BF16)
        src = P.dram["wv"].rearrange("p (c n) -> p c n", c=ND)[
            :, :, fh * 512:(fh + 1) * 512]
        dstw = wvh[:].rearrange("p (c n) -> p c n", c=ND)
        mk.dma("gpsimd", "dma_w",
               lambda e, dstw=dstw, src=src: e.dma_start(out=dstw, in_=src),
               w=[hk])
        for tt in range(NT):
            pb = tt % 4
            ps = P.psf[pb]
            pkey = f"psf{pb}"
            for c in range(ND):
                mk.op("tensor", lambda e, c=c, tt=tt, ps=ps, wvh=wvh: e.matmul(
                    ps[:], lnfm[:, c * T + tt * 128:c * T + (tt + 1) * 128],
                    wvh[:, c * 512:(c + 1) * 512],
                    start=(c == 0), stop=(c == ND - 1)),
                    r=[f"ln1:{c}", hk], w=[pkey])
            dst = v65[:].rearrange("p (t h c) -> p t h c", t=NT, h=16)[
                :, tt, fh * 8:(fh + 1) * 8, 0:64]
            srcv = ps[:].rearrange("p (h c) -> p h c", h=8)
            mk.op("vector", lambda e, dst=dst, srcv=srcv: e.tensor_copy(dst, srcv),
                  r=[pkey], w=[tk("v65a", tt)])
        slab.free(hk)

    def proj_fm(wname, wkey, outname, out, nhalf=1):
        hw = D // nhalf
        for half in range(nhalf):
            hk = f"{wkey}_h{half}"
            w = slab.alloc(hk, 128, ND * hw, BF16)
            src = P.dram[wname].rearrange("p (c n) -> p c n", c=ND)[
                :, :, half * hw:(half + 1) * hw]
            dstw = w[:].rearrange("p (c n) -> p c n", c=ND)
            mk.dma("gpsimd", "dma_w",
                   lambda e, dstw=dstw, src=src: e.dma_start(out=dstw, in_=src),
                   w=[hk])
            for co in range(half * ND // nhalf, (half + 1) * ND // nhalf):
                cof = co - half * ND // nhalf
                for t4 in range(4):
                    pb = (co * 4 + t4) % 4
                    ps = P.psf[pb]
                    pkey = f"psf{pb}"
                    for c in range(ND):
                        mk.op("tensor", lambda e, c=c, cof=cof, t4=t4, ps=ps, w=w:
                              e.matmul(
                            ps[:], w[:, c * hw + cof * 128:c * hw + cof * 128 + 128],
                            lnfm[:, c * T + t4 * 512:c * T + t4 * 512 + 512],
                            start=(c == 0), stop=(c == ND - 1)),
                            r=[f"ln1:{c}", hk], w=[pkey])
                    mk.op("vector", lambda e, co=co, t4=t4, ps=ps: e.tensor_copy(
                        out[:, co * T + t4 * 512:co * T + t4 * 512 + 512], ps[:]),
                        r=[pkey], w=[f"{outname}:{co}"])
            slab.free(hk)
        return out

    proj_fm("wk", "wk_a", "kfm_a", kfm, nhalf=2)
    proj_fm("wq", "wq_a", "qfm_a", qfm, nhalf=2)
    slab.free("ln1")

    ofm = slab.alloc("ofm_a", 128, ND * T, BF16,
                     keys=[f"ofm_a:{c}" for c in range(ND)])
    ebuf = slab.alloc("E_a", 128, 8 * 512, BF16,
                      keys=[f"E_a:{s}" for s in range(8)])
    rb = slab.alloc("rb_a", 128, 512, F32)
    attention_gen(
        P, list(range(16)), lambda h: 64 * (h % 2),
        qap=lambda h, q4: qfm[64 * (h % 2):64 * (h % 2) + 64,
                              (h // 2) * T + q4 * 512:(h // 2) * T + q4 * 512 + 512],
        kap=lambda h, st: kfm[64 * (h % 2):64 * (h % 2) + 64,
                              (h // 2) * T + st * 128:(h // 2) * T + (st + 1) * 128],
        vap=lambda h, st: v65[:, st * 1040 + h * 65:st * 1040 + h * 65 + 65],
        oap=lambda h, q4: ofm[64 * (h % 2):64 * (h % 2) + 64,
                              (h // 2) * T + q4 * 512:(h // 2) * T + q4 * 512 + 512],
        qk_rkeys=lambda h: [f"kfm_a:{h // 2}", f"qfm_a:{h // 2}"],
        v_rkey=lambda h, st: tk("v65a", st),
        o_wkey=lambda h: f"ofm_a:{h // 2}",
        ebuf=ebuf, ename="E_a", rb=rb, rbname="rb_a", scale=0.125)
    slab.free("E_a")
    slab.free("rb_a")
    slab.free("qfm_a")
    slab.free("kfm_a")
    slab.free("v65a")

    wo = load_weight_bf16(P, "wo", ND * D, key="wo_a")
    for i in range(NT):
        for f2 in range(2):
            pb = (i * 2 + f2) % 2
            ps = P.psf[pb]
            pkey = f"psf{pb}"
            for c in range(ND):
                mk.op("tensor", lambda e, c=c, i=i, f2=f2, ps=ps: e.matmul(
                    ps[:], ofm[:, c * T + i * 128:c * T + (i + 1) * 128],
                    wo[:, c * D + f2 * 512:c * D + f2 * 512 + 512],
                    start=(c == 0), stop=(c == ND - 1)),
                    r=[f"ofm_a:{c}", "wo_a"], w=[pkey])
            xt = P.xbuf[:, i * D + f2 * 512:i * D + f2 * 512 + 512]
            mk.op("vector", lambda e, xt=xt, ps=ps: e.tensor_tensor(
                xt, xt, ps[:], ALU.add),
                r=[pkey, tk("x", i)], w=[tk("x", i)])
    slab.free("wo_a")
    slab.free("ofm_a")



def attention_gen(P, heads, qb_fn, qap, kap, vap, oap, qk_rkeys, v_rkey,
                  o_wkey, ebuf, ename, rb, rbname, scale):
    mk = P.mk
    pend = None

    def tail(item):
        h, q4, pso, pokey = item
        mk.op("vector", lambda e, pso=pso: e.reciprocal(
            rb[64:65, 0:512], pso[64:65, :]),
            r=[pokey], w=[rbname + "_r"])
        p4 = P.psf[4]
        mk.op("tensor", lambda e: e.matmul(
            p4[0:64, :], P.ones_all_f[64:65, 0:64], rb[64:65, 0:512],
            start=True, stop=True),
            r=[rbname + "_r", "ones_all_f"], w=["psf4"])
        mk.op("scalar", lambda e, p4=p4: e.copy(rb[0:64, 0:512], p4[0:64, :]),
              r=["psf4"], w=[rbname + "_b"])
        mk.op("vector", lambda e, h=h, q4=q4, pso=pso: e.tensor_tensor(
            oap(h, q4), pso[0:64, :], rb[0:64, 0:512], ALU.mult),
            r=[pokey, rbname + "_b"], w=[o_wkey(h)])

    it = 0
    for h in heads:
        qb = qb_fn(h)
        for q4 in range(4):
            nst = 4 * (q4 + 1)
            po = 2 + it % 2
            it += 1
            pso = P.psf[po]
            pokey = f"psf{po}"
            nhalf = (nst + 7) // 8
            for half in range(nhalf):
                sts = list(range(half * 8, min(nst, half * 8 + 8)))
                for st in sts:
                    slot = st % 8
                    ps = P.psf[st % 2]
                    pkey = f"psf{st % 2}"
                    mk.op("tensor", lambda e, h=h, st=st, q4=q4, ps=ps, qb=qb:
                          e.matmul(ps[:], kap(h, st), qap(h, q4),
                                   start=True, stop=True, tile_position=(qb, 0)),
                          r=qk_rkeys(h), w=[pkey])
                    mk.op("scalar", lambda e, slot=slot, ps=ps: e.activation(
                        ebuf[:, slot * 512:(slot + 1) * 512], ps[:], AF.Exp,
                        scale=scale),
                        r=[pkey], w=[f"{ename}:{slot}"])
                    if st >= 4 * q4:
                        base = q4 * 512 - st * 128
                        mk.op("gpsimd", lambda e, slot=slot, base=base:
                              e.affine_select(
                            ebuf[:, slot * 512:(slot + 1) * 512],
                            ebuf[:, slot * 512:(slot + 1) * 512],
                            [[1, 512]], ALU.is_ge, 0.0, base=base,
                            channel_multiplier=-1),
                            r=[f"{ename}:{slot}"], w=[f"{ename}:{slot}"])
                if half == 0 and pend is not None:
                    tail(pend)
                    pend = None
                for st in sts:
                    slot = st % 8
                    mk.op("tensor", lambda e, h=h, st=st, slot=slot, pso=pso,
                          nst=nst: e.matmul(
                        pso[0:65, :], vap(h, st),
                        ebuf[:, slot * 512:(slot + 1) * 512],
                        start=(st == 0), stop=(st == nst - 1)),
                        r=[v_rkey(h, st), f"{ename}:{slot}"], w=[pokey])
            pend = (h, q4, pso, pokey)
    tail(pend)



def phase_ffn(P, tag):
    mk, slab = P.mk, P.slab
    ln = "ln" + tag
    uf = slab.alloc(tag + "_uf", 128, ND * T, BF16,
                    keys=[tag + f"_uf:{c}" for c in range(ND)])
    rwexp = slab.alloc(tag + "_rwexp", 128, ND * T, BF16,
                       keys=[tag + f"_rwexp:{c}" for c in range(ND)])
    wslot = slab.alloc(tag + "_wslot", 128, ND * D, BF16,
                       keys=[tag + "_wslot"])
    lnfm, cols = layernorm_fm(P, "x", P.xbuf, ln, eps=1e-5, add_back=False,
                              keep_cols=True)
    ck = ln + "_c"

    mk.op("vector", lambda e: e.scalar_tensor_tensor(
        cols[:, 80:96], cols[:, 48:64], -1e-5, cols[:, 48:64],
        ALU.mult, ALU.mult), r=[ck], w=[ck])
    mk.op("vector", lambda e: e.tensor_scalar(
        cols[:, 80:96], cols[:, 80:96], float(D), float(D), ALU.mult, ALU.add),
        r=[ck], w=[ck])
    mk.op("scalar", lambda e: e.activation(cols[:, 80:96], cols[:, 80:96],
                                           AF.Sqrt), r=[ck], w=[ck])
    mk.op("vector", lambda e: e.reciprocal(cols[:, 80:96], cols[:, 80:96]),
          r=[ck], w=[ck])

    simf = slab.alloc(tag + "_sim", 128, 128, F32)
    mk.dma("sync", "dma_x", lambda e: e.dma_start(out=simf[:], in_=P.dram["sim"]),
           w=[tag + "_sim"])
    grow = slab.alloc(tag + "_grow", 1, 16, F32)
    mk.dma("sync", "dma_x", lambda e: e.dma_start(out=grow[:], in_=P.dram["gates"]),
           w=[tag + "_grow"])
    s2 = slab.alloc(tag + "_s2", 128, 128, F32)
    mk.op("vector", lambda e: e.tensor_mul(s2[:], simf[:], simf[:]),
          r=[tag + "_sim"], w=[tag + "_s2"])
    p5 = P.psf[5]
    for c in range(ND):
        mk.op("tensor", lambda e, c=c: e.matmul(
            p5[0:1, 0:16], P.ones_all_f[:, 0:1], s2[:, c * 16:(c + 1) * 16],
            start=(c == 0), stop=(c == ND - 1)),
            r=[tag + "_s2", "ones_all_f"], w=["psf5"])
    rs = slab.alloc(tag + "_rs", 1, 32, F32)
    mk.op("scalar", lambda e: e.activation(rs[0:1, 0:16], p5[0:1, 0:16],
                                           AF.Sqrt), r=["psf5"], w=[tag + "_rs"])
    mk.op("vector", lambda e: e.reciprocal(rs[0:1, 0:16], rs[0:1, 0:16]),
          r=[tag + "_rs"], w=[tag + "_rs"])
    mk.op("scalar", lambda e: e.activation(rs[0:1, 16:32], grow[0:1, :],
                                           AF.Sigmoid),
          r=[tag + "_grow"], w=[tag + "_rs"])
    mk.op("tensor", lambda e: e.matmul(
        p5[:, 0:32], P.ones_row_f[0:1, :], rs[0:1, 0:32],
        start=True, stop=True), r=[tag + "_rs", "ones_row_f"], w=["psf5"])
    brep = slab.alloc(tag + "_brep", 128, 32, F32)
    mk.op("vector", lambda e: e.tensor_copy(brep[:], p5[:, 0:32]),
          r=["psf5"], w=[tag + "_brep"])
    slab.free(tag + "_s2")

    ln32 = [slab.alloc(tag + f"_l32{p}", 128, D, F32) for p in range(2)]
    lnT = [slab.alloc(tag + f"_lT{p}", 128, ND * 128, F32) for p in range(2)]
    gs = slab.alloc(tag + "_gs", 128, 6 * 16 + 8, F32)
    rwt = slab.alloc(tag + "_rwt", 128, NT * 16, F32,
                     keys=[tag + f"_rwt:{i}" for i in range(NT)])
    rwb = slab.alloc(tag + "_rwb", 128, 16, BF16)
    rwfm = slab.alloc(tag + "_rwfm", 16, T, BF16)
    gk = tag + "_gs"
    mk.op("vector", lambda e: e.memset(gs[:, 80:96], -60.0), w=[gk])
    for i in range(NT):
        par = i % 2
        l32 = ln32[par]
        lT = lnT[par]
        l32k = tag + f"_l32{par}"
        lTk = tag + f"_lT{par}"
        xt = P.xbuf[:, i * D:(i + 1) * D]
        rstd = cols[:, 48 + i:49 + i]
        negm = cols[:, 64 + i:65 + i]
        rn = cols[:, 80 + i:81 + i]
        mk.op("vector", lambda e, xt=xt, rstd=rstd, negm=negm, l32=l32:
              e.tensor_scalar(l32[:], xt, rstd, negm, ALU.mult, ALU.add),
              r=[tk("x", i), ck], w=[l32k])
        for g in range(2):
            pt = P.psf[g]
            ptk = f"psf{g}"
            for k2 in range(4):
                c = g * 4 + k2
                mk.op("tensor", lambda e, c=c, k2=k2, pt=pt, l32=l32: e.transpose(
                    pt[:, k2 * 128:(k2 + 1) * 128], l32[:, c * 128:(c + 1) * 128],
                    P.ident_f[:]),
                    r=[l32k, "ident_f"], w=[ptk])
            mk.op("vector", lambda e, g=g, pt=pt, lT=lT: e.tensor_copy(
                lT[:, g * 512:(g + 1) * 512], pt[:]),
                r=[ptk], w=[lTk])
        p2 = P.psf[2]
        for c in range(ND):
            mk.op("tensor", lambda e, c=c, lT=lT: e.matmul(
                p2[:, 0:16], lT[:, c * 128:(c + 1) * 128],
                simf[:, c * 16:(c + 1) * 16],
                start=(c == 0), stop=(c == ND - 1)),
                r=[lTk, tag + "_sim"], w=["psf2"])
        mk.op("vector", lambda e, rn=rn: e.scalar_tensor_tensor(
            gs[:, 0:16], p2[:, 0:16], rn, brep[:, 0:16], ALU.mult, ALU.mult),
            r=["psf2", ck, tag + "_brep"], w=[gk])
        mk.op("vector", lambda e: e.tensor_sub(gs[:, 0:16], gs[:, 0:16],
                                               brep[:, 16:32]),
              r=[gk, tag + "_brep"], w=[gk])
        mk.op("vector", lambda e: e.reduce_max(gs[:, 96:97], gs[:, 0:16], axis=AX),
              r=[gk], w=[gk])
        mk.op("vector", lambda e: e.tensor_single_scalar(
            gs[:, 16:32], gs[:, 0:16], 0.0, ALU.max), r=[gk], w=[gk])
        mk.op("vector", lambda e: e.tensor_single_scalar(
            gs[:, 98:99], gs[:, 96:97], 0.0, ALU.is_le), r=[gk], w=[gk])
        mk.op("vector", lambda e: e.tensor_scalar(
            gs[:, 32:48], gs[:, 0:16], gs[:, 96:97], None, ALU.is_ge),
            r=[gk], w=[gk])
        mk.op("vector", lambda e: e.scalar_tensor_tensor(
            gs[:, 32:48], gs[:, 32:48], -1e30, gs[:, 0:16], ALU.mult, ALU.add),
            r=[gk], w=[gk])
        mk.op("vector", lambda e: e.reduce_max(gs[:, 97:98], gs[:, 32:48], axis=AX),
              r=[gk], w=[gk])
        mk.op("vector", lambda e: e.tensor_scalar(
            gs[:, 48:64], gs[:, 0:16], gs[:, 97:98], None, ALU.is_ge),
            r=[gk], w=[gk])
        mk.op("vector", lambda e: e.tensor_scalar(
            gs[:, 48:64], gs[:, 48:64], gs[:, 98:99], None, ALU.mult),
            r=[gk], w=[gk])
        mk.op("vector", lambda e: e.tensor_single_scalar(
            gs[:, 32:48], gs[:, 0:16], 0.0, ALU.is_gt), r=[gk], w=[gk])
        mk.op("vector", lambda e: e.tensor_max(gs[:, 32:48], gs[:, 32:48],
                                               gs[:, 48:64]),
              r=[gk], w=[gk])
        mk.op("vector", lambda e: e.tensor_scalar_add(gs[:, 16:32],
                                                      gs[:, 16:32], 60.0),
              r=[gk], w=[gk])
        mk.op("vector", lambda e: e.tensor_mul(gs[:, 0:16], gs[:, 32:48],
                                               gs[:, 16:32]),
              r=[gk], w=[gk])
        mk.op("vector", lambda e: e.tensor_scalar_add(gs[:, 0:16],
                                                      gs[:, 0:16], -60.0),
              r=[gk], w=[gk])
        mk.op("vector", lambda e: e.reduce_max(gs[:, 99:100], gs[:, 0:16],
                                               axis=AX, negate=True),
              r=[gk], w=[gk])
        mk.op("scalar", lambda e: e.activation(gs[:, 64:80], gs[:, 0:16],
                                               AF.Exp, bias=gs[:, 99:100]),
              r=[gk], w=[gk])
        mk.op("vector", lambda e: e.reduce_sum(gs[:, 100:101], gs[:, 64:80],
                                               axis=AX), r=[gk], w=[gk])
        mk.op("vector", lambda e: e.reciprocal(gs[:, 101:102], gs[:, 100:101]),
              r=[gk], w=[gk])
        mk.op("vector", lambda e, i=i: e.tensor_scalar(
            rwt[:, i * 16:(i + 1) * 16], gs[:, 64:80], gs[:, 101:102], None,
            ALU.mult),
            r=[gk], w=[tag + f"_rwt:{i}"])
        mk.op("vector", lambda e, i=i: e.tensor_copy(rwb[:],
                                                     rwt[:, i * 16:(i + 1) * 16]),
              r=[tag + f"_rwt:{i}"], w=[tag + "_rwb"])
        pbb = P.psb[i % 2]
        pbk = f"psb{i % 2}"
        mk.op("tensor", lambda e, pbb=pbb: e.transpose(
            pbb[0:16, 0:128], rwb[:], P.ident_bf[:]),
            r=[tag + "_rwb", "ident_bf"], w=[pbk])
        mk.op("vector", lambda e, i=i, pbb=pbb: e.tensor_copy(
            rwfm[0:16, i * 128:(i + 1) * 128], pbb[0:16, 0:128]),
            r=[pbk], w=[tag + "_rwfm"])
    slab.free(tag + "_l320")
    slab.free(tag + "_l321")
    slab.free(tag + "_lT0")
    slab.free(tag + "_lT1")
    slab.free(tag + "_gs")
    slab.free(tag + "_sim")
    slab.free(tag + "_grow")
    slab.free(tag + "_rs")
    slab.free(tag + "_rwb")

    ppat = slab.alloc(tag + "_ppat", 16, 1024, BF16)
    mk.dma("gpsimd", "dma_w", lambda e: e.dma_start(out=ppat[:], in_=P.dram["ppat"]),
           w=[tag + "_ppat"])
    for c in range(ND):
        for t4 in range(4):
            ps = P.psf[(c * 4 + t4) % 4]
            pkey = f"psf{(c * 4 + t4) % 4}"
            mk.op("tensor", lambda e, c=c, t4=t4, ps=ps: e.matmul(
                ps[:], ppat[0:16, c * 128:(c + 1) * 128],
                rwfm[0:16, t4 * 512:(t4 + 1) * 512],
                start=True, stop=True),
                r=[tag + "_ppat", tag + "_rwfm"], w=[pkey])
            mk.op("vector", lambda e, c=c, t4=t4, ps=ps: e.tensor_copy(
                rwexp[:, c * T + t4 * 512:c * T + t4 * 512 + 512], ps[:]),
                r=[pkey], w=[tag + f"_rwexp:{c}"])
    slab.free(tag + "_ppat")

    spat = slab.alloc(tag + "_spat", 128, ND * 64, BF16)
    mk.dma("gpsimd", "dma_w", lambda e: e.dma_start(out=spat[:], in_=P.dram["spat"]),
           w=[tag + "_spat"])
    qk = slab.alloc(tag + "_qk", 128, 2 * T, BF16)
    v65f = slab.alloc(tag + "_v65", 128, NT * 65, BF16,
                      keys=[tag + f"_v65:{t}" for t in range(NT)])
    ones_v = v65f[:].rearrange("p (t c) -> p t c", t=NT)[:, :, 64:65]
    mk.op("vector", lambda e: e.memset(ones_v, 1.0),
          w=[tag + f"_v65:{t}" for t in range(NT)])

    for pi, wname in enumerate(("qall", "kall", "vall")):
        wk_ = tag + "_wslot"
        w = wslot
        mk.dma("gpsimd", "dma_w",
               lambda e, w=w, wname=wname: e.dma_start(out=w[:], in_=P.dram[wname]),
               w=[wk_])
        for co in range(ND):
            for t4 in range(4):
                ps = P.psf[(co * 4 + t4) % 4]
                pkey = f"psf{(co * 4 + t4) % 4}"
                for c in range(ND):
                    mk.op("tensor", lambda e, c=c, co=co, t4=t4, ps=ps, w=w:
                          e.matmul(
                        ps[:], w[:, c * D + co * 128:c * D + co * 128 + 128],
                        lnfm[:, c * T + t4 * 512:c * T + t4 * 512 + 512],
                        start=(c == 0), stop=(c == ND - 1)),
                        r=[f"{ln}:{c}", wk_], w=[pkey])
                mk.op("vector", lambda e, co=co, t4=t4, ps=ps: e.tensor_tensor(
                    uf[:, co * T + t4 * 512:co * T + t4 * 512 + 512], ps[:],
                    rwexp[:, co * T + t4 * 512:co * T + t4 * 512 + 512],
                    ALU.mult),
                    r=[pkey, tag + f"_rwexp:{co}"], w=[tag + f"_uf:{co}"])
        if pi < 2:
            for t4 in range(4):
                ps = P.psf[2 + t4 % 2]
                pkey = f"psf{2 + t4 % 2}"
                for c in range(ND):
                    mk.op("tensor", lambda e, c=c, t4=t4, ps=ps: e.matmul(
                        ps[0:64, :], spat[:, c * 64:(c + 1) * 64],
                        uf[:, c * T + t4 * 512:c * T + t4 * 512 + 512],
                        start=(c == 0), stop=(c == ND - 1)),
                        r=[tag + f"_uf:{c}", tag + "_spat"], w=[pkey])
                mk.op("vector", lambda e, pi=pi, t4=t4, ps=ps: e.tensor_copy(
                    qk[0:64, pi * T + t4 * 512:pi * T + t4 * 512 + 512],
                    ps[0:64, :]),
                    r=[pkey], w=[tag + "_qk"])
        else:
            for tt in range(NT):
                ps = P.psf[2 + tt % 2]
                pkey = f"psf{2 + tt % 2}"
                for c in range(ND):
                    mk.op("tensor", lambda e, c=c, tt=tt, ps=ps: e.matmul(
                        ps[:, 0:64], uf[:, c * T + tt * 128:c * T + (tt + 1) * 128],
                        spat[:, c * 64:(c + 1) * 64],
                        start=(c == 0), stop=(c == ND - 1)),
                        r=[tag + f"_uf:{c}", tag + "_spat"], w=[pkey])
                mk.op("vector", lambda e, tt=tt, ps=ps: e.tensor_copy(
                    v65f[:, tt * 65:tt * 65 + 64], ps[:, 0:64]),
                    r=[pkey], w=[tag + f"_v65:{tt}"])
    slab.free(ln)
    slab.free(tag + "_spat")

    ebuf = slab.alloc(tag + "_E", 128, 8 * 512, BF16,
                      keys=[tag + f"_E:{s}" for s in range(8)])
    rb = slab.alloc(tag + "_rb", 128, 512, F32)
    af = slab.alloc(tag + "_af", 128, T, BF16)
    attention_gen(
        P, [0], lambda h: 0,
        qap=lambda h, q4: qk[0:64, q4 * 512:q4 * 512 + 512],
        kap=lambda h, st: qk[0:64, T + st * 128:T + (st + 1) * 128],
        vap=lambda h, st: v65f[:, st * 65:(st + 1) * 65],
        oap=lambda h, q4: af[0:64, q4 * 512:q4 * 512 + 512],
        qk_rkeys=lambda h: [tag + "_qk"],
        v_rkey=lambda h, st: tag + f"_v65:{st}",
        o_wkey=lambda h: tag + "_af",
        ebuf=ebuf, ename=tag + "_E", rb=rb, rbname=tag + "_rb", scale=0.125)
    slab.free(tag + "_E")
    slab.free(tag + "_rb")
    slab.free(tag + "_qk")
    slab.free(tag + "_v65")

    spatT = slab.alloc(tag + "_spatT", 64, 1024, BF16)
    mk.dma("gpsimd", "dma_w", lambda e: e.dma_start(out=spatT[:],
                                                    in_=P.dram["spatT"]),
           w=[tag + "_spatT"])
    for c in range(ND):
        for t4 in range(4):
            ps = P.psf[(c * 4 + t4) % 4]
            pkey = f"psf{(c * 4 + t4) % 4}"
            mk.op("tensor", lambda e, c=c, t4=t4, ps=ps: e.matmul(
                ps[:], spatT[0:64, c * 128:(c + 1) * 128],
                af[0:64, t4 * 512:t4 * 512 + 512],
                start=True, stop=True),
                r=[tag + "_spatT", tag + "_af"], w=[pkey])
            mk.op("vector", lambda e, c=c, t4=t4, ps=ps: e.tensor_tensor(
                uf[:, c * T + t4 * 512:c * T + t4 * 512 + 512], ps[:],
                rwexp[:, c * T + t4 * 512:c * T + t4 * 512 + 512], ALU.mult),
                r=[pkey, tag + f"_rwexp:{c}", tag + f"_uf:{c}"],
                w=[tag + f"_uf:{c}"])
    slab.free(tag + "_spatT")
    slab.free(tag + "_af")

    wo_ = tag + "_wslot"
    w = wslot
    mk.dma("gpsimd", "dma_w", lambda e, w=w: e.dma_start(out=w[:],
                                                         in_=P.dram["oall"]),
           w=[wo_])
    for i in range(NT):
        for f2 in range(2):
            ps = P.psf[(i * 2 + f2) % 4]
            pkey = f"psf{(i * 2 + f2) % 4}"
            for c in range(ND):
                mk.op("tensor", lambda e, c=c, i=i, f2=f2, ps=ps, w=w: e.matmul(
                    ps[:], uf[:, c * T + i * 128:c * T + (i + 1) * 128],
                    w[:, c * D + f2 * 512:c * D + f2 * 512 + 512],
                    start=(c == 0), stop=(c == ND - 1)),
                    r=[tag + f"_uf:{c}", wo_], w=[pkey])
            xt = P.xbuf[:, i * D + f2 * 512:i * D + f2 * 512 + 512]
            mk.op("vector", lambda e, xt=xt, ps=ps: e.tensor_tensor(
                xt, xt, ps[:], ALU.add),
                r=[pkey, tk("x", i)], w=[tk("x", i)])
    slab.free(tag + "_wslot")
    slab.free(tag + "_uf")
    slab.free(tag + "_rwexp")
    slab.free(tag + "_rwfm")
    slab.free(tag + "_rwt")
    slab.free(tag + "_brep")
    slab.free(ln + "_c")



DIN = 2048
NH = 32
HD = 64
DST = 128


def phase_attnb_setup(P):
    mk, slab = P.mk, P.slab
    xb = slab.alloc("xb", 128, NT * D, F32, keys=[tk("xb", i) for i in range(NT)])
    cols = slab.alloc("b_lnc", 128, 96, F32)
    ck = "b_lnc"
    sc = slab.alloc("b_lnsc", 128, D, BF16)
    for i in range(NT):
        xt = P.xbuf[:, i * D:(i + 1) * D]
        mk.op("scalar", lambda e, xt=xt, i=i: e.activation(
            sc[:], xt, AF.Square, accum_out=cols[:, 16 + i:17 + i]),
            r=[tk("x", i)], w=[ck, "b_lnsc"])
        mk.op("vector", lambda e, xt=xt, i=i: e.reduce_sum(
            cols[:, 0 + i:1 + i], xt, axis=AX),
            r=[tk("x", i)], w=[ck])
    mk.op("vector", lambda e: e.tensor_scalar_mul(cols[:, 0:16], cols[:, 0:16],
                                                  1.0 / D), r=[ck], w=[ck])
    mk.op("vector", lambda e: e.tensor_scalar(
        cols[:, 32:48], cols[:, 16:32], 1.0 / D, 1e-5, ALU.mult, ALU.add),
        r=[ck], w=[ck])
    mk.op("vector", lambda e: e.scalar_tensor_tensor(
        cols[:, 64:80], cols[:, 0:16], -1.0, cols[:, 0:16], ALU.mult, ALU.mult),
        r=[ck], w=[ck])
    mk.op("vector", lambda e: e.tensor_add(cols[:, 32:48], cols[:, 32:48],
                                           cols[:, 64:80]), r=[ck], w=[ck])
    mk.op("scalar", lambda e: e.activation(cols[:, 32:48], cols[:, 32:48],
                                           AF.Sqrt), r=[ck], w=[ck])
    mk.op("vector", lambda e: e.reciprocal(cols[:, 48:64], cols[:, 32:48]),
          r=[ck], w=[ck])
    mk.op("vector", lambda e: e.scalar_tensor_tensor(
        cols[:, 64:80], cols[:, 0:16], -1.0, cols[:, 48:64], ALU.mult, ALU.mult),
        r=[ck], w=[ck])
    for i in range(NT):
        xt = P.xbuf[:, i * D:(i + 1) * D]
        xbt = xb[:, i * D:(i + 1) * D]
        mk.op("vector", lambda e, xt=xt, xbt=xbt, i=i: e.tensor_scalar(
            xbt, xt, cols[:, 48 + i:49 + i], cols[:, 64 + i:65 + i],
            ALU.mult, ALU.add),
            r=[tk("x", i), ck], w=[tk("xb", i)])
    slab.free("b_lnc")
    slab.free("b_lnsc")
    P.xb = xb
    for h in range(4):
        src = P.xbuf[:, h * 4 * D:(h + 1) * 4 * D]
        mk.dma("sync", "dma_o",
               lambda e, h=h, src=src: e.dma_start(
                   out=P.dram["xspill"][:, h * 4 * D:(h + 1) * 4 * D], in_=src),
               r=[tk("x", i) for i in range(h * 4, h * 4 + 4)],
               w=[f"xspill:{h}"])
    slab.free("x")
    P.xbuf = None


def reload_x(P):
    mk, slab = P.mk, P.slab
    P.xbuf = slab.alloc("x", 128, NT * D, F32,
                        keys=[tk("x", i) for i in range(NT)])
    for h in range(4):
        dst = P.xbuf[:, h * 4 * D:(h + 1) * 4 * D]
        mk.dma("sync", "dma_x",
               lambda e, h=h, dst=dst: e.dma_start(
                   in_=P.dram["xspill"][:, h * 4 * D:(h + 1) * 4 * D], out=dst),
               r=[f"xspill:{h}"],
               w=[tk("x", i) for i in range(h * 4, h * 4 + 4)])


def rms_stats(P, buf, bufkey, name, eps):
    mk, slab = P.mk, P.slab
    cols = slab.alloc(name + "_c", 128, 32, F32)
    ck = name + "_c"
    sc = slab.alloc(name + "_sc", 128, D, BF16)
    for i in range(NT):
        xt = buf[:, i * D:(i + 1) * D]
        mk.op("scalar", lambda e, xt=xt, i=i: e.activation(
            sc[:], xt, AF.Square, accum_out=cols[:, 0 + i:1 + i]),
            r=[tk(bufkey, i)], w=[ck, name + "_sc"])
    mk.op("vector", lambda e: e.tensor_scalar(
        cols[:, 16:32], cols[:, 0:16], 1.0 / D, eps, ALU.mult, ALU.add),
        r=[ck], w=[ck])
    mk.op("scalar", lambda e: e.activation(cols[:, 16:32], cols[:, 16:32],
                                           AF.Sqrt), r=[ck], w=[ck])
    mk.op("vector", lambda e: e.reciprocal(cols[:, 16:32], cols[:, 16:32]),
          r=[ck], w=[ck])
    slab.free(name + "_sc")
    return cols


def rms_chunk_fm_gen(P, buf, bufkey, cols, ck, name, tiles, fm, fmname, fmT):
    mk, slab = P.mk, P.slab
    sc = slab.alloc(name + "_nsc", 128, D, BF16)
    for j, i in enumerate(tiles):
        xt = buf[:, i * D:(i + 1) * D]
        mk.op("vector", lambda e, xt=xt, i=i: e.tensor_scalar(
            sc[:], xt, cols[:, 16 + i:17 + i], None, ALU.mult),
            r=[tk(bufkey, i), ck], w=[name + "_nsc"])
        yield
        transpose_tile_to_fm(P, name + "_nsc",
                             lambda c, s=sc: s[:, c * 128:(c + 1) * 128],
                             fm, fmname, j, fmT=fmT)
        yield
    slab.free(name + "_nsc")


def rms_chunk_fm(P, buf, bufkey, cols, ck, name, tiles, fm, fmname, fmT):
    for _ in rms_chunk_fm_gen(P, buf, bufkey, cols, ck, name, tiles, fm,
                              fmname, fmT):
        pass


def win_production_gen(P, tc, rcols, rck, A32, halo, pslist, TC, NSUB):
    mk, slab = P.mk, P.slab
    xb = P.xb
    tts = [tc * NSUB + s for s in range(NSUB)]
    r1 = slab.alloc("b_r1fm", 128, ND * TC, BF16,
                    keys=[f"b_r1fm:{c}" for c in range(ND)])
    for _ in rms_chunk_fm_gen(P, xb, "xb", rcols, rck, "b_r1n", tts, r1,
                              "b_r1fm", TC):
        yield
    xbcraw = slab.alloc("b_xbcraw", 128, 18 * (TC + 3), BF16,
                        keys=["b_xbcraw"])
    P.cur_xbcraw = xbcraw
    hv = xbcraw[:].rearrange("p (c t) -> p c t", c=18)[:, :, 0:3]
    sv = halo[:].rearrange("p (c t) -> p c t", c=18)
    mk.op("vector", lambda e, hv=hv, sv=sv: e.tensor_copy(hv, sv),
          r=["b_halo"], w=["b_xbcraw"])
    dtraw = slab.alloc("b_dtraw", 32, TC, F32, keys=["b_dtraw"])
    P.cur_dtraw = dtraw
    wib = [slab.alloc(f"b_wib{p}", 128, ND * 128, BF16) for p in range(2)]
    nsb = 19
    def issue_dma(k):
        m = 32 if k == 18 else 128
        c0 = 2048 + 128 * k
        par = k % 2
        wbk = f"b_wib{par}"
        src = P.dram["win"].rearrange("p (c n) -> p c n", c=ND)[:, :, c0:c0 + m]
        dstw = wib[par][:].rearrange("p (c n) -> p c n", c=ND)[:, :, 0:m]
        mk.dma("gpsimd", "dma_w",
               lambda e, dstw=dstw, src=src: e.dma_start(out=dstw, in_=src),
               w=[wbk])
    issue_dma(0)
    yield
    for k in range(nsb):
        if k + 1 < nsb:
            issue_dma(k + 1)
        yield
        m = 32 if k == 18 else 128
        par = k % 2
        wbk = f"b_wib{par}"
        ps = pslist[k % len(pslist)]
        pkey = f"psf{P.psf.index(ps)}"
        for c in range(ND):
            mk.op("tensor", lambda e, c=c, ps=ps, par=par, m=m: e.matmul(
                ps[0:m, 0:TC], wib[par][:, c * 128:c * 128 + m],
                r1[:, c * TC:(c + 1) * TC],
                start=(c == 0), stop=(c == ND - 1)),
                r=[f"b_r1fm:{c}", wbk], w=[pkey])
        if k == 18:
            mk.op("vector", lambda e, ps=ps: e.tensor_scalar(
                dtraw[0:32, :], ps[0:32, 0:TC], A32[0:32, 1:2], None,
                ALU.add),
                r=[pkey, "b_A"], w=["b_dtraw"])
        else:
            dst = xbcraw[:, k * (TC + 3) + 3:(k + 1) * (TC + 3)]
            mk.op("vector", lambda e, dst=dst, ps=ps: e.tensor_copy(
                dst, ps[0:128, 0:TC]),
                r=[pkey], w=["b_xbcraw"])
        yield
    dump(P, f"xbcraw{tc}", xbcraw[:, 0:9216].rearrange("p (i f) -> p i f",
                                                       f=1024), ["b_xbcraw"])
    slab.free("b_r1fm")
    slab.free("b_wib0")
    slab.free("b_wib1")


def phase_attnb_mamba(P):
    mk, slab = P.mk, P.slab
    xb = P.xb
    TC = 512
    NCH = T // TC
    NSUB = TC // 128
    rcols = rms_stats(P, xb, "xb", "b_r1", 1e-6)
    rck = "b_r1_c"

    A32 = slab.alloc("b_A", 32, 4, F32)
    mk.dma("sync", "dma_x", lambda e: e.dma_start(out=A32[0:32, 0:1],
                                                  in_=P.dram["alog"]), w=["b_A"])
    mk.dma("sync", "dma_x", lambda e: e.dma_start(out=A32[0:32, 1:2],
                                                  in_=P.dram["dtb"]), w=["b_A"])
    mk.op("scalar", lambda e: e.activation(A32[0:32, 0:1], A32[0:32, 0:1],
                                           AF.Exp), r=["b_A"], w=["b_A"])
    mk.op("vector", lambda e: e.tensor_scalar_mul(A32[0:32, 0:1],
                                                  A32[0:32, 0:1], -1.0),
          r=["b_A"], w=["b_A"])
    dmrep = slab.alloc("b_dmrep", 128, 32, F32)
    drow = slab.alloc("b_dmrow", 1, 32, F32)
    mk.dma("sync", "dma_x", lambda e: e.dma_start(out=drow[:], in_=P.dram["dm"]),
           w=["b_dmrow"])
    p5 = P.psf[5]
    mk.op("tensor", lambda e: e.matmul(p5[:, 0:32], P.ones_row_f[0:1, :],
                                       drow[0:1, :], start=True, stop=True),
          r=["b_dmrow", "ones_row_f"], w=["psf5"])
    mk.op("vector", lambda e: e.tensor_copy(dmrep[:], p5[:, 0:32]),
          r=["psf5"], w=["b_dmrep"])
    convw = slab.alloc("b_convw", 128, 72, F32)
    mk.dma("sync", "dma_x", lambda e: e.dma_start(out=convw[:],
                                                  in_=P.dram["convw"]),
           w=["b_convw"])
    convb = slab.alloc("b_convb", 128, 18, F32)
    mk.dma("sync", "dma_x", lambda e: e.dma_start(out=convb[:],
                                                  in_=P.dram["convb"]),
           w=["b_convb"])
    mnorm = slab.alloc("b_mnorm", 128, 16, F32)
    mk.dma("sync", "dma_x", lambda e: e.dma_start(out=mnorm[:],
                                                  in_=P.dram["mnorm"]),
           w=["b_mnorm"])
    zeros32 = slab.alloc("b_z32", 32, 128, F32)
    mk.op("vector", lambda e: e.memset(zeros32[:], 0.0), w=["b_z32"])
    dg32 = slab.alloc("b_dg32", 32, 32, F32)

    stf = slab.alloc("b_stf", 128, NH * HD, F32,
                     keys=[f"b_stf:{h}" for h in range(NH)])
    stb = slab.alloc("b_stb", 128, NH * HD, BF16,
                     keys=[f"b_stb:{h}" for h in range(NH)])
    mk.op("vector", lambda e: e.memset(stf[:], 0.0),
          w=[f"b_stf:{h}" for h in range(NH)])
    mk.op("vector", lambda e: e.memset(stb[:], 0.0),
          w=[f"b_stb:{h}" for h in range(NH)])
    halo = slab.alloc("b_halo", 128, 18 * 3, BF16)
    mk.op("vector", lambda e: e.memset(halo[:], 0.0), w=["b_halo"])

    for tc in range(NCH):
        tts = [tc * NSUB + s for s in range(NSUB)]
        if True:
            for _ in win_production_gen(P, tc, rcols, rck, A32, halo,
                                        [P.psf[0], P.psf[1], P.psf[2],
                                         P.psf[3]], TC, NSUB):
                pass
        xbcraw = P.cur_xbcraw
        dtraw = P.cur_dtraw
        dump(P, f"xbcrawpre{tc}", xbcraw[:, 0:9216].rearrange(
            "p (i f) -> p i f", f=1024), ["b_xbcraw"])
        dump(P, f"convw{tc}", convw[:, 0:72], ["b_convw"])
        dts = slab.alloc("b_dts", 32, 3 * TC, F32, keys=["b_dts"])
        mk.op("scalar", lambda e: e.activation(dts[0:32, 2 * TC:3 * TC],
                                               dtraw[0:32, :], AF.Abs),
              r=["b_dtraw"], w=["b_dts"])
        mk.op("scalar", lambda e: e.activation(dts[0:32, 2 * TC:3 * TC],
                                               dts[0:32, 2 * TC:3 * TC],
                                               AF.Exp, scale=-1.0),
              r=["b_dts"], w=["b_dts"])
        mk.op("scalar", lambda e: e.activation(dts[0:32, 2 * TC:3 * TC],
                                               dts[0:32, 2 * TC:3 * TC],
                                               AF.Ln, bias=P.cc[0:32, 2:3]),
              r=["b_dts", "cc"], w=["b_dts"])
        mk.op("vector", lambda e: e.tensor_single_scalar(
            dts[0:32, 0:TC], dtraw[0:32, :], 0.0, ALU.max),
            r=["b_dtraw"], w=["b_dts"])
        mk.op("vector", lambda e: e.tensor_add(
            dts[0:32, 0:TC], dts[0:32, 0:TC], dts[0:32, 2 * TC:3 * TC]),
            r=["b_dts"], w=["b_dts"])
        mk.op("vector", lambda e: e.tensor_scalar(
            dts[0:32, TC:2 * TC], dts[0:32, 0:TC], A32[0:32, 0:1], None,
            ALU.mult),
            r=["b_dts", "b_A"], w=["b_dts"])
        slab.free("b_dtraw")
        lamb = slab.alloc("b_lam", 32, TC, F32, keys=["b_lam"])
        for s in range(NSUB):
            mk.op("vector", lambda e, s=s: e.tensor_tensor_scan(
                lamb[0:32, s * 128:(s + 1) * 128],
                dts[0:32, TC + s * 128:TC + (s + 1) * 128],
                zeros32[0:32, :], 0.0, ALU.add, ALU.add),
                r=["b_dts", "b_z32"], w=["b_lam"])

        dump(P, f"dts{tc}", dts[0:32, :], ["b_dts"])
        xbcpost = slab.alloc("b_xbcpost", 128, 18 * TC, BF16, keys=["b_xbcpost"])
        for row in range(18):
            ac = P.psf[2 + row % 2]
            ak = f"psf{2 + row % 2}"
            base = row * (TC + 3)
            mk.op("vector", lambda e, base=base, row=row, ac=ac: e.tensor_scalar(
                ac[:, 0:TC], xbcraw[:, base:base + TC],
                convw[:, row * 4:row * 4 + 1], None, ALU.mult),
                r=["b_xbcraw", "b_convw"], w=[ak])
            for j2 in range(1, 4):
                mk.op("vector", lambda e, base=base, row=row, j2=j2, ac=ac:
                      e.scalar_tensor_tensor(
                    ac[:, 0:TC], xbcraw[:, base + j2:base + j2 + TC],
                    convw[:, row * 4 + j2:row * 4 + j2 + 1], ac[:, 0:TC],
                    ALU.mult, ALU.add),
                    r=["b_xbcraw", "b_convw", ak], w=[ak])
            mk.op("scalar", lambda e, row=row, ac=ac: e.activation(
                xbcpost[:, row * TC:(row + 1) * TC], ac[:, 0:TC], AF.Silu,
                bias=convb[:, row:row + 1]),
                r=[ak, "b_convb"], w=["b_xbcpost"])
        dump(P, f"xbcpost{tc}", xbcpost[:, 0:9216].rearrange("p (i f) -> p i f", f=1024),
             ["b_xbcpost"])
        hv2 = halo[:].rearrange("p (c t) -> p c t", c=18)
        sv2 = xbcraw[:].rearrange("p (c t) -> p c t", c=18)[:, :, TC:TC + 3]
        mk.op("vector", lambda e, hv2=hv2, sv2=sv2: e.tensor_copy(hv2, sv2),
              r=["b_xbcraw"], w=["b_halo"])
        slab.free("b_xbcraw")

        bc = slab.alloc("b_bc", 128, 2 * TC, BF16, keys=["b_bc"])
        mk.op("vector", lambda e: e.tensor_copy(bc[:, 0:TC],
                                                xbcpost[:, 16 * TC:17 * TC]),
              r=["b_xbcpost"], w=["b_bc"])
        mk.op("vector", lambda e: e.tensor_copy(bc[:, TC:2 * TC],
                                                xbcpost[:, 17 * TC:18 * TC]),
              r=["b_xbcpost"], w=["b_bc"])
        xtm = slab.alloc("b_xtm", 128, NSUB * DIN, BF16, keys=["b_xtm"])
        for s in range(NSUB):
            for g in range(0, 16, 4):
                pb = P.psb[(g // 4) % 2]
                pbk = f"psb{(g // 4) % 2}"
                for k2 in range(4):
                    c = g + k2
                    mk.op("tensor", lambda e, c=c, k2=k2, s=s, pb=pb: e.transpose(
                        pb[:, k2 * 128:(k2 + 1) * 128],
                        xbcpost[:, c * TC + s * 128:c * TC + (s + 1) * 128],
                        P.ident_bf[:]),
                        r=["b_xbcpost", "ident_bf"], w=[pbk])
                dst = xtm[:, s * DIN + g * 128:s * DIN + (g + 4) * 128]
                mk.op("vector", lambda e, dst=dst, pb=pb: e.tensor_copy(
                    dst, pb[:, 0:512]),
                    r=[pbk], w=["b_xtm"])
        slab.free("b_xbcpost")
        mts = slab.alloc("b_mt", 128, NSUB * 128, BF16, keys=["b_mt"])
        btm = slab.alloc("b_btm", 128, NSUB * 128, BF16, keys=["b_btm"])
        for s in range(NSUB):
            ps = P.psf[s % 2]
            pkey = f"psf{s % 2}"
            mk.op("tensor", lambda e, s=s, ps=ps: e.matmul(
                ps[:, 0:128], bc[:, s * 128:(s + 1) * 128],
                bc[:, TC + s * 128:TC + (s + 1) * 128],
                start=True, stop=True),
                r=["b_bc"], w=[pkey])
            mk.op("vector", lambda e, s=s, ps=ps: e.tensor_copy(
                mts[:, s * 128:(s + 1) * 128], ps[:, 0:128]),
                r=[pkey], w=["b_mt"])
            pb = P.psb[s % 2]
            pbk = f"psb{s % 2}"
            mk.op("tensor", lambda e, s=s, pb=pb: e.transpose(
                pb[:, 0:128], bc[:, s * 128:(s + 1) * 128], P.ident_bf[:]),
                r=["b_bc", "ident_bf"], w=[pbk])
            mk.op("vector", lambda e, s=s, pb=pb: e.tensor_copy(
                btm[:, s * 128:(s + 1) * 128], pb[:, 0:128]),
                r=[pbk], w=["b_btm"])

        sccols = slab.alloc("b_scc", 128, NSUB * 160, F32, keys=["b_scc"])
        for s in range(NSUB):
            off = s * 160
            ps4 = P.psf[4]
            for (srcoff, dstoff) in ((2 * TC, 0), (0, 32)):
                mk.op("tensor", lambda e, s=s, srcoff=srcoff, ps4=ps4:
                      e.transpose(
                    ps4[:, 0:32],
                    (lamb if srcoff else dts)[0:32, (srcoff % (2 * TC)) + s * 128:
                                              (srcoff % (2 * TC)) + (s + 1) * 128],
                    P.ident_f[0:32, 0:32]),
                    r=["b_dts", "b_lam", "ident_f"], w=["psf4"])
                mk.op("vector", lambda e, off=off, dstoff=dstoff, ps4=ps4:
                      e.tensor_copy(sccols[:, off + dstoff:off + dstoff + 32],
                                    ps4[:, 0:32]),
                      r=["psf4"], w=["b_scc"])
            mk.op("vector", lambda e, off=off: e.tensor_scalar_mul(
                sccols[:, off + 64:off + 96], sccols[:, off:off + 32], -1.0),
                r=["b_scc"], w=["b_scc"])
            mk.op("vector", lambda e, s=s: e.tensor_scalar(
                dg32[0:32, 0:32], P.ident_f[0:32, 0:32],
                lamb[0:32, (s + 1) * 128 - 1:(s + 1) * 128],
                None, ALU.mult),
                r=["ident_f", "b_lam"], w=["b_dg32"])
            mk.op("tensor", lambda e: e.matmul(
                P.psf[4][:, 0:32], P.ones_all_f[0:32, 0:128], dg32[0:32, 0:32],
                start=True, stop=True),
                r=["b_dg32", "ones_all_f"], w=["psf4"])
            mk.op("vector", lambda e, off=off: e.tensor_sub(
                sccols[:, off + 96:off + 128], P.psf[4][:, 0:32],
                sccols[:, off:off + 32]),
                r=["psf4", "b_scc"], w=["b_scc"])
            mk.op("scalar", lambda e, off=off: e.activation(
                sccols[:, off + 96:off + 128], sccols[:, off + 96:off + 128],
                AF.Exp), r=["b_scc"], w=["b_scc"])
            mk.op("vector", lambda e, off=off: e.tensor_mul(
                sccols[:, off + 96:off + 128], sccols[:, off + 96:off + 128],
                sccols[:, off + 32:off + 64]),
                r=["b_scc"], w=["b_scc"])
            mk.op("scalar", lambda e, off=off: e.activation(
                sccols[:, off + 128:off + 160], P.psf[4][:, 0:32], AF.Exp),
                r=["psf4"], w=["b_scc"])

        slab.free("b_dts")
        lamhb = slab.alloc("b_lamhb", 32, 2 * TC, BF16, keys=["b_lamhb"])
        mk.op("vector", lambda e: e.tensor_copy(
            lamhb[0:32, 0:TC], lamb[0:32, :]), r=["b_lam"], w=["b_lamhb"])
        mk.op("vector", lambda e: e.tensor_sub(
            lamhb[0:32, TC:2 * TC], lamb[0:32, :], lamhb[0:32, 0:TC]),
            r=["b_lam", "b_lamhb"], w=["b_lamhb"])
        yfm = slab.alloc("b_yfm", 128, 16 * TC, BF16,
                         keys=[f"b_yfm:{c}" for c in range(16)])
        wsc = [slab.alloc(f"b_wsc{p}", 128, 512, BF16) for p in range(2)]
        csc = [slab.alloc(f"b_csc{p}", 128, 512, BF16) for p in range(2)]
        xsc = [slab.alloc(f"b_xsc{p}", 128, 2 * 256, BF16) for p in range(2)]
        lsl = [slab.alloc(f"b_lsl{p}", 32, 1024, BF16) for p in range(2)]
        dmd = slab.alloc("b_dmd", 128, 512, BF16)
        hot4 = slab.alloc("b_hot4", 32, 512, BF16)
        negsF = slab.alloc("b_negsF", 32, 512, F32)
        negsel = slab.alloc("b_negs", 32, 512, BF16)
        wingen = None

        def rep4(ap_, blk):
            from concourse.ap import AP as _AP
            return _AP(ap_.tensor, ap_.offset, [ap_.ap[0], [0, 4]] + [ap_.ap[1]])

        for quad in range(8):
            h0 = quad * 4
            cpar = quad % 2
            mk.op("gpsimd", lambda e, h0=h0: e.iota(
                negsF[0:32, :].rearrange("p (a b) -> p a b", a=4),
                [[-1, 4], [0, 128]], base=-h0, channel_multiplier=1,
                allow_small_or_imprecise_dtypes=True),
                r=["b_negsF"], w=["b_negsF"])
            mk.op("vector", lambda e: e.tensor_single_scalar(
                hot4[0:32, :], negsF[0:32, :], 0.0, ALU.is_equal),
                r=["b_negsF"], w=["b_hot4"])
            mk.op("vector", lambda e: e.tensor_scalar_mul(
                negsel[0:32, :], hot4[0:32, :], -1.0),
                r=["b_hot4"], w=["b_negs"])
            dmv = dmrep[:, h0:h0 + 4]
            mk.op("vector", lambda e, dmv=dmv: e.tensor_tensor(
                dmd[:].rearrange("p (a b) -> p a b", a=4),
                AP_IDENT4(P), AP_COLREP(dmv, 128), ALU.mult),
                r=["ident_bf", "b_dmrep"], w=["b_dmd"])
            for s in range(NSUB):
                off = s * 160
                lam = lamb[0:32, s * 128:(s + 1) * 128]
                wt, ct, xs, ls = wsc[cpar], csc[cpar], xsc[cpar], lsl[cpar]
                wtk, ctk, xsk = f"b_wsc{cpar}", f"b_csc{cpar}", f"b_xsc{cpar}"
                lsk = f"b_lsl{cpar}"
                pd, pdk = P.psf[4 + cpar], f"psf{4 + cpar}"
                py, pyk = P.psf[2 + cpar], f"psf{2 + cpar}"
                pc, pck = P.psf[cpar], f"psf{cpar}"
                PCOFF = 0
                lamhi = lamhb[0:32, s * 128:(s + 1) * 128]
                lamlo = lamhb[0:32, TC + s * 128:TC + (s + 1) * 128]
                mk.op("vector", lambda e, lamhi=lamhi, ls=ls: e.tensor_tensor(
                    ls[0:32, 0:512].rearrange("p (a b) -> p a b", a=4),
                    rep4(lamhi, 128), hot4[0:32, :].rearrange(
                        "p (a b) -> p a b", a=4), ALU.mult),
                    r=["b_lamhb", "b_hot4"], w=[lsk])
                mk.op("vector", lambda e, lamlo=lamlo, ls=ls: e.tensor_tensor(
                    ls[0:32, 512:1024].rearrange("p (a b) -> p a b", a=4),
                    rep4(lamlo, 128), hot4[0:32, :].rearrange(
                        "p (a b) -> p a b", a=4), ALU.mult),
                    r=["b_lamhb", "b_hot4"], w=[lsk])
                mk.op("tensor", lambda e, ls=ls, pd=pd: e.matmul(
                    pd[:], P.ones_all_b[0:32, 0:128], ls[0:32, 0:512],
                    start=True, stop=False),
                    r=[lsk, "ones_all_b"], w=[pdk])
                mk.op("tensor", lambda e, ls=ls, pd=pd: e.matmul(
                    pd[:], P.ones_all_b[0:32, 0:128], ls[0:32, 512:1024],
                    start=False, stop=False),
                    r=[lsk, "ones_all_b", pdk], w=[pdk])
                mk.op("scalar", lambda e, ct=ct, pd=pd: e.activation(
                    ct[:], pd[:], AF.Exp), r=[pdk], w=[ctk])
                cslice = bc[:, TC + s * 128:TC + (s + 1) * 128]
                mk.op("gpsimd", lambda e, ct=ct, cslice=cslice: e.tensor_tensor(
                    ct[:].rearrange("p (a b) -> p a b", a=4),
                    ct[:].rearrange("p (a b) -> p a b", a=4),
                    rep4(cslice, 128), ALU.mult),
                    r=[ctk, "b_bc"], w=[ctk])
                mk.op("tensor", lambda e, lamhi=lamhi, pd=pd: e.matmul(
                    pd[:], lamhi, negsel[0:32, :], start=False, stop=False),
                    r=["b_lamhb", "b_negs", pdk], w=[pdk])
                mk.op("tensor", lambda e, lamlo=lamlo, pd=pd: e.matmul(
                    pd[:], lamlo, negsel[0:32, :], start=False, stop=True),
                    r=["b_lamhb", "b_negs", pdk], w=[pdk])
                mk.op("scalar", lambda e, wt=wt, pd=pd: e.activation(
                    wt[:], pd[:], AF.Exp), r=[pdk], w=[wtk])
                mslice = mts[:, s * 128:(s + 1) * 128]
                mk.op("gpsimd", lambda e, wt=wt, mslice=mslice: e.tensor_tensor(
                    wt[:].rearrange("p (a b) -> p a b", a=4),
                    wt[:].rearrange("p (a b) -> p a b", a=4),
                    rep4(mslice, 128), ALU.mult),
                    r=[wtk, "b_mt"], w=[wtk])
                mk.op("gpsimd", lambda e, wt=wt: e.affine_select(
                    wt[:].rearrange("p (a b) -> p a b", a=4),
                    wt[:].rearrange("p (a b) -> p a b", a=4),
                    [[0, 4], [1, 128]], ALU.is_ge, 0.0, base=0,
                    channel_multiplier=-1),
                    r=[wtk], w=[wtk])
                xq = xtm[:, s * DIN + h0 * HD:s * DIN + (h0 + 4) * HD]
                dtv = sccols[:, off + 32 + h0:off + 36 + h0]
                ebv = sccols[:, off + 96 + h0:off + 100 + h0]
                mk.op("vector", lambda e, xq=xq, dtv=dtv, xs=xs: e.tensor_tensor(
                    xs[:, 0:256].rearrange("p (a b) -> p a b", a=4),
                    xq.rearrange("p (a b) -> p a b", a=4),
                    AP_COLREP(dtv, HD), ALU.mult),
                    r=["b_xtm", "b_scc"], w=[xsk])
                mk.op("vector", lambda e, xq=xq, ebv=ebv, xs=xs: e.tensor_tensor(
                    xs[:, 256:512].rearrange("p (a b) -> p a b", a=4),
                    xq.rearrange("p (a b) -> p a b", a=4),
                    AP_COLREP(ebv, HD), ALU.mult),
                    r=["b_xtm", "b_scc"], w=[xsk])
                for hh in range(4):
                    h = h0 + hh
                    yo = 64 * (h % 2)
                    pair = hh // 2
                    mk.op("tensor", lambda e, hh=hh, xs=xs, wt=wt, py=py, yo=yo:
                          e.matmul(
                        py[yo:yo + HD, 0:128], xs[:, hh * HD:(hh + 1) * HD],
                        wt[:, hh * 128:(hh + 1) * 128],
                        start=True, stop=False, tile_position=(0, yo)),
                        r=[xsk, wtk], w=[pyk])
                    mk.op("tensor", lambda e, s=s, h=h, hh=hh, py=py, yo=yo:
                          e.matmul(
                        py[yo:yo + HD, 0:128],
                        xtm[:, s * DIN + h * HD:s * DIN + (h + 1) * HD],
                        dmd[:, hh * 128:(hh + 1) * 128], start=False, stop=False,
                        tile_position=(0, yo)),
                        r=["b_xtm", "b_dmd"], w=[pyk])
                    mk.op("tensor", lambda e, h=h, hh=hh, ct=ct, py=py, yo=yo:
                          e.matmul(
                        py[yo:yo + HD, 0:128], stb[:, h * HD:(h + 1) * HD],
                        ct[:, hh * 128:(hh + 1) * 128],
                        start=False, stop=True, tile_position=(0, yo)),
                        r=[f"b_stb:{h}", ctk], w=[pyk])
                    mk.op("tensor", lambda e, s=s, hh=hh, xs=xs, pc=pc: e.matmul(
                        pc[:, PCOFF + hh * HD:PCOFF + (hh + 1) * HD],
                        btm[:, s * 128:(s + 1) * 128],
                        xs[:, 256 + hh * HD:256 + (hh + 1) * HD],
                        start=True, stop=True),
                        r=["b_btm", xsk], w=[pck])
                    mk.op("vector", lambda e, h=h, hh=hh, off=off, pc=pc:
                          e.scalar_tensor_tensor(
                        stf[:, h * HD:(h + 1) * HD], stf[:, h * HD:(h + 1) * HD],
                        sccols[:, off + 128 + h:off + 129 + h],
                        pc[:, PCOFF + hh * HD:PCOFF + (hh + 1) * HD],
                        ALU.mult, ALU.add),
                        r=[f"b_stf:{h}", "b_scc", pck], w=[f"b_stf:{h}"])
                    if hh % 2 == 1:
                        yc = (h * HD) // 128
                        mk.op("vector", lambda e, s=s, yc=yc, py=py:
                              e.tensor_copy(
                            yfm[:, yc * TC + s * 128:yc * TC + (s + 1) * 128],
                            py[:, 0:128]),
                            r=[pyk], w=[f"b_yfm:{yc}"])
                mk.op("gpsimd", lambda e, h0=h0: e.tensor_copy(
                    stb[:, h0 * HD:(h0 + 4) * HD], stf[:, h0 * HD:(h0 + 4) * HD]),
                    r=[f"b_stf:{h}" for h in range(h0, h0 + 4)],
                    w=[f"b_stb:{h}" for h in range(h0, h0 + 4)])
                if wingen is not None:
                    next(wingen, None)
                    next(wingen, None)
        if wingen is not None:
            for _ in wingen:
                pass
        for nm in ("b_wsc0", "b_wsc1", "b_csc0", "b_csc1", "b_xsc0", "b_xsc1",
                   "b_lsl0", "b_lsl1", "b_hot4", "b_negs", "b_negsF",
                   "b_lamhb", "b_lam",
                   "b_dmd", "b_scc", "b_mt", "b_btm", "b_xtm", "b_bc"):
            slab.free(nm)

        r1 = slab.alloc("b_r1fm", 128, ND * TC, BF16,
                        keys=[f"b_r1fm:{c}" for c in range(ND)])
        rms_chunk_fm(P, xb, "xb", rcols, rck, "b_r1n2", tts, r1, "b_r1fm", TC)
        zs = slab.alloc("b_zs", 128, TC, BF16, keys=["b_zs"])
        for zb in range(4):
            zwk = f"b_zw_{tc}_{zb}"
            zw = slab.alloc(zwk, 128, ND * 512, BF16)
            src2 = P.dram["win"].rearrange("p (c n) -> p c n", c=ND)[
                :, :, zb * 512:(zb + 1) * 512]
            dstw = zw[:].rearrange("p (c n) -> p c n", c=ND)
            mk.dma("gpsimd", "dma_w",
                   lambda e, dstw=dstw, src2=src2: e.dma_start(out=dstw, in_=src2),
                   w=[zwk])
            for co in range(4):
                row = zb * 4 + co
                ps = P.psf[co % 4]
                pkey = f"psf{co % 4}"
                for c in range(ND):
                    mk.op("tensor", lambda e, c=c, co=co, ps=ps, zw=zw: e.matmul(
                        ps[:, 0:TC], zw[:, c * 512 + co * 128:c * 512 + co * 128 + 128],
                        r1[:, c * TC:(c + 1) * TC],
                        start=(c == 0), stop=(c == ND - 1)),
                        r=[f"b_r1fm:{c}", zwk], w=[pkey])
                mk.op("scalar", lambda e, ps=ps: e.activation(
                    zs[:], ps[:, 0:TC], AF.Silu), r=[pkey], w=["b_zs"])
                mk.op("vector", lambda e, row=row: e.tensor_mul(
                    yfm[:, row * TC:(row + 1) * TC],
                    yfm[:, row * TC:(row + 1) * TC], zs[:]),
                    r=["b_zs", f"b_yfm:{row}"], w=[f"b_yfm:{row}"])
            slab.free(zwk)
        slab.free("b_zs")
        slab.free("b_r1fm")

        dump(P, "scc", sccols[:, :], ["b_scc"]) if getattr(P, "dbg_tag", None) == "scc" else None
        dump(P, f"yfm{tc}", yfm[:].rearrange("p (i f) -> p i f", f=1024),
             [f"b_yfm:{c}" for c in range(16)])
        ysq = slab.alloc("b_ysq", 128, TC, BF16, keys=["b_ysq"])
        rrow = slab.alloc("b_rr", 128, 8, F32, keys=["b_rr"])
        ps5 = P.psf[5]
        for c in range(16):
            mk.op("vector", lambda e, c=c: e.tensor_mul(
                ysq[:], yfm[:, c * TC:(c + 1) * TC], yfm[:, c * TC:(c + 1) * TC]),
                r=[f"b_yfm:{c}"], w=["b_ysq"])
            mk.op("tensor", lambda e, c=c: e.matmul(
                ps5[0:1, 0:TC], P.ones_all_b[:, 0:1],
                ysq[:], start=(c == 0), stop=(c == 15)),
                r=["b_ysq", "ones_all_b"], w=["psf5"])
        rtmp = slab.alloc("b_rtmp", 1, TC, F32, keys=["b_rtmp"])
        mk.op("vector", lambda e: e.tensor_copy(rtmp[0:1, :], ps5[0:1, 0:TC]),
              r=["psf5"], w=["b_rtmp"])
        for s in range(NSUB):
            mk.op("tensor", lambda e, s=s: e.transpose(
                P.psf[4][:, 0:1], rtmp[0:1, s * 128:(s + 1) * 128],
                P.ident_f[0:1, 0:1]),
                r=["b_rtmp", "ident_f"], w=["psf4"])
            mk.op("vector", lambda e, s=s: e.tensor_copy(
                rrow[:, s:s + 1], P.psf[4][:, 0:1]),
                r=["psf4"], w=["b_rr"])
        slab.free("b_rtmp")
        mk.op("vector", lambda e: e.tensor_scalar(
            rrow[:, 0:NSUB], rrow[:, 0:NSUB], 1.0 / DIN, 1e-5, ALU.mult,
            ALU.add),
            r=["b_rr"], w=["b_rr"])
        mk.op("scalar", lambda e: e.activation(rrow[:, 0:NSUB], rrow[:, 0:NSUB],
                                               AF.Sqrt), r=["b_rr"], w=["b_rr"])
        mk.op("vector", lambda e: e.reciprocal(rrow[:, 0:NSUB], rrow[:, 0:NSUB]),
              r=["b_rr"], w=["b_rr"])
        slab.free("b_ysq")

        wo = [None, None]
        for hf in range(2):
            wok = f"b_wout_{tc}_{hf}"
            wo[hf] = slab.alloc(wok, 128, 8 * 1024, BF16)
            src = P.dram["wout"][:, hf * 8192:(hf + 1) * 8192]
            mk.dma("gpsimd", "dma_w",
                   lambda e, hf=hf, src=src: e.dma_start(out=wo[hf][:], in_=src),
                   w=[wok])
            for c in range(8):
                mk.op("vector", lambda e, hf=hf, c=c: e.tensor_scalar(
                    wo[hf][:, c * 1024:(c + 1) * 1024],
                    wo[hf][:, c * 1024:(c + 1) * 1024],
                    mnorm[:, hf * 8 + c:hf * 8 + c + 1], None, ALU.mult),
                    r=[wok, "b_mnorm"], w=[wok])
        for s in range(NSUB):
            for f2 in range(2):
                pi = s * 2 + f2
                ps = P.psf[pi % 4]
                pkey = f"psf{pi % 4}"
                for kc in range(16):
                    hf = kc // 8
                    mk.op("tensor", lambda e, s=s, f2=f2, kc=kc, hf=hf, ps=ps:
                          e.matmul(
                        ps[:], yfm[:, kc * TC + s * 128:kc * TC + (s + 1) * 128],
                        wo[hf][:, (kc % 8) * 1024 + f2 * 512:
                               (kc % 8) * 1024 + f2 * 512 + 512],
                        start=(kc == 0), stop=(kc == 15)),
                        r=[f"b_yfm:{kc}", f"b_wout_{tc}_{hf}"], w=[pkey])
                i = tts[s]
                xbt = xb[:, i * D + f2 * 512:i * D + f2 * 512 + 512]
                mk.op("vector", lambda e, s=s, xbt=xbt, ps=ps: e.scalar_tensor_tensor(
                    xbt, ps[:], rrow[:, s:s + 1], xbt, ALU.mult, ALU.add),
                    r=[pkey, "b_rr", tk("xb", tts[s])], w=[tk("xb", tts[s])])
        slab.free(f"b_wout_{tc}_0")
        slab.free(f"b_wout_{tc}_1")
        slab.free("b_rr")
        slab.free("b_yfm")
    for nm in ("b_r1_c", "b_A", "b_dmrep", "b_dmrow", "b_convw", "b_convb",
               "b_mnorm", "b_z32", "b_dg32", "b_stf", "b_stb", "b_halo"):
        slab.free(nm)


def conv3_piece(P, padbuf, padkey, taps, tapkey, outbuf, outkey, facc, fkey,
                rows, tapcol_fn, biascol, npiece, TT=T):
    mk = P.mk
    for pc in range(npiece):
        base = pc * 512
        mk.op("vector", lambda e, base=base: e.tensor_scalar(
            facc[rows, 0:512], padbuf[rows, base:base + 512], tapcol_fn(0),
            None, ALU.mult),
            r=[padkey, tapkey], w=[fkey])
        for j in (1, 2):
            mk.op("vector", lambda e, base=base, j=j: e.scalar_tensor_tensor(
                facc[rows, 0:512], padbuf[rows, base + j:base + j + 512],
                tapcol_fn(j), facc[rows, 0:512], ALU.mult, ALU.add),
                r=[padkey, tapkey, fkey], w=[fkey])
        mk.op("vector", lambda e, base=base: e.tensor_scalar(
            outbuf[rows, base:base + 512], facc[rows, 0:512], biascol, None,
            ALU.add),
            r=[fkey, tapkey], w=[outkey])


def phase_attnb_attn(P):
    mk, slab = P.mk, P.slab
    xb = P.xb
    rcols = rms_stats(P, xb, "xb", "b_r2", 1e-6)
    rck = "b_r2_c"
    r2 = slab.alloc("c2_r2", 128, ND * T, BF16,
                    keys=[f"c2_r2:{c}" for c in range(ND)])
    rms_chunk_fm(P, xb, "xb", rcols, rck, "c2_r2n", list(range(NT)), r2,
                 "c2_r2", T)
    qcw = slab.alloc("c2_qcw", 128, 24, F32)
    mk.dma("sync", "dma_x", lambda e: e.dma_start(out=qcw[:], in_=P.dram["qconvw"]),
           w=["c2_qcw"])
    qcb = slab.alloc("c2_qcb", 128, 8, F32)
    mk.dma("sync", "dma_x", lambda e: e.dma_start(out=qcb[:], in_=P.dram["qconvb"]),
           w=["c2_qcb"])
    kvc = slab.alloc("c2_kvc", 128, 4, F32)
    mk.dma("sync", "dma_x", lambda e: e.dma_start(out=kvc[:], in_=P.dram["kvconv"]),
           w=["c2_kvc"])
    kkc = slab.alloc("c2_kkc", 128, 4, F32)
    mk.dma("sync", "dma_x", lambda e: e.dma_start(out=kkc[:], in_=P.dram["kkconv"]),
           w=["c2_kkc"])

    pad = slab.alloc("c2_pad", 128, T + 2, BF16, keys=["c2_pad"])
    facc = slab.alloc("c2_facc", 128, 512, F32, keys=["c2_facc"])
    qpost = slab.alloc("c2_qpost", 128, T, BF16, keys=["c2_qpost"])

    def proj_pair_to_pad(colslice_fn, rkeys):
        for t4 in range(4):
            ps = P.psf[t4 % 4]
            pkey = f"psf{t4 % 4}"
            for c in range(ND):
                mk.op("tensor", lambda e, c=c, t4=t4, ps=ps: e.matmul(
                    ps[:], colslice_fn(c),
                    r2[:, c * T + t4 * 512:c * T + t4 * 512 + 512],
                    start=(c == 0), stop=(c == ND - 1)),
                    r=[f"c2_r2:{c}"] + rkeys, w=[pkey])
            mk.op("vector", lambda e, t4=t4, ps=ps: e.tensor_copy(
                pad[:, 2 + t4 * 512:2 + (t4 + 1) * 512], ps[:]),
                r=[pkey], w=["c2_pad"])

    mk.op("vector", lambda e: e.memset(pad[:, 0:2], 0.0), w=["c2_pad"])

    for p in range(8):
        wk_ = f"c2_wq{p}"
        wsl = slab.alloc(wk_, 128, ND * 128, BF16)
        srcw = P.dram["wqkv"].rearrange("p (c n) -> p c n", c=ND)[
            :, :, p * 128:(p + 1) * 128]
        dstw = wsl[:].rearrange("p (c n) -> p c n", c=ND)
        mk.dma("gpsimd", "dma_w",
               lambda e, dstw=dstw, srcw=srcw: e.dma_start(out=dstw, in_=srcw),
               w=[wk_])
        proj_pair_to_pad(lambda c, wsl=wsl: wsl[:, c * 128:(c + 1) * 128], [wk_])
        conv3_piece(P, pad, "c2_pad", qcw, "c2_qcw", qpost, "c2_qpost",
                    facc, "c2_facc", slice(0, 128),
                    lambda j, p=p: qcw[:, p * 3 + j:p * 3 + j + 1],
                    qcb[:, p:p + 1], 4)
        mk.dma("gpsimd", "dma_w",
               lambda e, p=p: e.dma_start(out=P.dram["qscr"][:, p * T:(p + 1) * T],
                                          in_=qpost[:]),
               r=["c2_qpost"], w=[f"c2_qs:{p}"])
        slab.free(wk_)

    wk_ = "c2_wkv"
    wsl = slab.alloc(wk_, 128, ND * 128, BF16)
    srcw = P.dram["wqkv"].rearrange("p (c n) -> p c n", c=ND)[:, :, 1024:1152]
    dstw = wsl[:].rearrange("p (c n) -> p c n", c=ND)
    mk.dma("gpsimd", "dma_w",
           lambda e, dstw=dstw, srcw=srcw: e.dma_start(out=dstw, in_=srcw),
           w=[wk_])
    kvpost = slab.alloc("c2_kvpost", 128, T, BF16, keys=["c2_kvpost"])
    proj_pair_to_pad(lambda c: wsl[:, c * 128:(c + 1) * 128], [wk_])
    conv3_piece(P, pad, "c2_pad", kvc, "c2_kvc", kvpost, "c2_kvpost",
                facc, "c2_facc", slice(0, 128),
                lambda j: kvc[:, j:j + 1], kvc[:, 3:4], 4)
    kdup2 = slab.alloc("c2_kdup2", 128, T, BF16, keys=["c2_kdup2"])
    for t4 in range(4):
        ps = P.psf[t4 % 2]
        pkey = f"psf{t4 % 2}"
        for c in range(ND):
            mk.op("tensor", lambda e, c=c, t4=t4, ps=ps: e.matmul(
                ps[64:128, :], wsl[:, c * 128:c * 128 + 64],
                r2[:, c * T + t4 * 512:c * T + t4 * 512 + 512],
                start=(c == 0), stop=(c == ND - 1), tile_position=(0, 64)),
                r=[f"c2_r2:{c}", wk_], w=[pkey])
        mk.op("vector", lambda e, t4=t4, ps=ps: e.tensor_copy(
            pad[64:128, 2 + t4 * 512:2 + (t4 + 1) * 512], ps[64:128, :]),
            r=[pkey], w=["c2_pad"])
    conv3_piece(P, pad, "c2_pad", kkc, "c2_kkc", kdup2, "c2_kdup2",
                facc, "c2_facc", slice(64, 128),
                lambda j: kkc[64:128, j:j + 1], kkc[64:128, 3:4], 4)
    slab.free(wk_)
    slab.free("c2_qpost")
    slab.free("c2_r2")
    slab.free("b_r2_c")
    slab.free("c2_pad")
    slab.free("c2_facc")
    slab.free("c2_qcw")
    slab.free("c2_qcb")
    slab.free("c2_kvc")
    slab.free("c2_kkc")

    v65b = slab.alloc("c2_v65", 128, NT * 65, BF16,
                      keys=[f"c2_v65:{t}" for t in range(NT)])
    ones_v = v65b[:].rearrange("p (t c) -> p t c", t=NT)[:, :, 64:65]
    mk.op("vector", lambda e: e.memset(ones_v, 1.0),
          w=[f"c2_v65:{t}" for t in range(NT)])
    for tt in range(NT):
        pb = P.psb[tt % 2]
        pbk = f"psb{tt % 2}"
        mk.op("tensor", lambda e, tt=tt, pb=pb: e.transpose(
            pb[:, 0:64], kvpost[64:128, tt * 128:(tt + 1) * 128],
            P.ident_bf[64:128, 64:128]),
            r=["c2_kvpost", "ident_bf"], w=[pbk])
        mk.op("vector", lambda e, tt=tt, pb=pb: e.tensor_copy(
            v65b[:, tt * 65:tt * 65 + 64], pb[:, 0:64]),
            r=[pbk], w=[f"c2_v65:{tt}"])

    ob = slab.alloc("c2_ob", 128, ND * T, BF16,
                    keys=[f"c2_ob:{c}" for c in range(ND)])
    ebuf = slab.alloc("c2_E", 128, 8 * 512, BF16,
                      keys=[f"c2_E:{s}" for s in range(8)])
    rb = slab.alloc("c2_rb", 128, 512, F32)
    qpair = slab.alloc("c2_qpair", 128, T, BF16, keys=["c2_qpair"])
    for p in range(8):
        mk.dma("gpsimd", "dma_w",
               lambda e, p=p: e.dma_start(out=qpair[:],
                                          in_=P.dram["qscr"][:, p * T:(p + 1) * T]),
               r=[f"c2_qs:{p}"], w=["c2_qpair"])
        attention_gen(
            P, [2 * p, 2 * p + 1], lambda h: 64 * (h % 2),
            qap=lambda h, q4: qpair[64 * (h % 2):64 * (h % 2) + 64,
                                    q4 * 512:q4 * 512 + 512],
            kap=lambda h, st: (kvpost if h % 2 == 0 else kdup2)[
                64 * (h % 2):64 * (h % 2) + 64, st * 128:(st + 1) * 128],
            vap=lambda h, st: v65b[:, st * 65:(st + 1) * 65],
            oap=lambda h, q4: ob[64 * (h % 2):64 * (h % 2) + 64,
                                 (h // 2) * T + q4 * 512:(h // 2) * T + q4 * 512 + 512],
            qk_rkeys=lambda h: ["c2_qpair", "c2_kvpost", "c2_kdup2"],
            v_rkey=lambda h, st: f"c2_v65:{st}",
            o_wkey=lambda h: f"c2_ob:{h // 2}",
            ebuf=ebuf, ename="c2_E", rb=rb, rbname="c2_rb", scale=0.125)
    slab.free("c2_qpair")
    slab.free("c2_rb")
    slab.free("c2_E")
    slab.free("c2_v65")
    slab.free("c2_kdup2")
    slab.free("c2_kvpost")

    wc = slab.alloc("c2_wc", 128, ND * D, BF16)
    mk.dma("gpsimd", "dma_w", lambda e: e.dma_start(out=wc[:],
                                                    in_=P.dram["wcproj"]),
           w=["c2_wc"])
    for i in range(NT):
        for f2 in range(2):
            ps = P.psf[(i * 2 + f2) % 4]
            pkey = f"psf{(i * 2 + f2) % 4}"
            for c in range(ND):
                mk.op("tensor", lambda e, c=c, i=i, f2=f2, ps=ps: e.matmul(
                    ps[:], ob[:, c * T + i * 128:c * T + (i + 1) * 128],
                    wc[:, c * D + f2 * 512:c * D + f2 * 512 + 512],
                    start=(c == 0), stop=(c == ND - 1)),
                    r=[f"c2_ob:{c}", "c2_wc"], w=[pkey])
            xbt = xb[:, i * D + f2 * 512:i * D + f2 * 512 + 512]
            mk.op("vector", lambda e, xbt=xbt, ps=ps: e.tensor_tensor(
                xbt, xbt, ps[:], ALU.add),
                r=[pkey, tk("xb", i)], w=[tk("xb", i)])
    slab.free("c2_wc")
    slab.free("c2_ob")


def phase_attnb_rwkv(P):
    mk, slab = P.mk, P.slab
    xb = P.xb
    TC3 = 256
    NCH = T // TC3
    rcols = rms_stats(P, xb, "xb", "b_r3", 1e-6)
    rck = "b_r3_c"
    wrec = slab.alloc("c3_wrec", 128, ND * D, BF16)
    mk.dma("gpsimd", "dma_w", lambda e: e.dma_start(out=wrec[:],
                                                    in_=P.dram["wrec"]),
           w=["c3_wrec"])
    maa = slab.alloc("c3_maa", 128, 16, F32)
    mk.dma("sync", "dma_x", lambda e: e.dma_start(out=maa[:, 0:8],
                                                  in_=P.dram["maak"]),
           w=["c3_maa"])
    mk.dma("sync", "dma_x", lambda e: e.dma_start(out=maa[:, 8:16],
                                                  in_=P.dram["maar"]),
           w=["c3_maa"])
    h3c = slab.alloc("c3_h3", 128, ND * (TC3 + 1), BF16, keys=["c3_h3"])
    hsc = slab.alloc("c3_hsc", 128, D, BF16, keys=["c3_hsc"])
    asc = slab.alloc("c3_asc", 128, TC3, F32, keys=["c3_asc"])
    xkc = slab.alloc("c3_xk", 128, ND * TC3, BF16,
                     keys=[f"c3_xk:{c}" for c in range(ND)])
    xrc = slab.alloc("c3_xr", 128, ND * TC3, BF16,
                     keys=[f"c3_xr:{c}" for c in range(ND)])
    sigc = slab.alloc("c3_sig", 128, 2 * D, BF16, keys=["c3_sig"])
    kfr = slab.alloc("c3_kfr", 128, 2 * TC3, BF16, keys=["c3_kfr"])
    kfb = slab.alloc("c3_kfb", 128, 2 * TC3, BF16, keys=["c3_kfb"])
    dsc = slab.alloc("c3_dsc", 128, 512, F32, keys=["c3_dsc"])
    wkb = [slab.alloc(f"c3_wk{i}", 128, ND * 256, BF16) for i in range(2)]
    wvb = [slab.alloc(f"c3_wv{i}", 128, 2 * D, BF16) for i in range(2)]
    h3v = h3c[:].rearrange("p (c t) -> p c t", c=ND)

    for tcc in range(NCH):
        tiles = [tcc * 2, tcc * 2 + 1]
        if tcc == 0:
            mk.op("vector", lambda e: e.memset(h3v[:, :, 0:1], 0.0),
                  r=["c3_h3"], w=["c3_h3"])
        else:
            mk.op("vector", lambda e: e.tensor_copy(h3v[:, :, 0:1],
                                                    h3v[:, :, TC3:TC3 + 1]),
                  r=["c3_h3"], w=["c3_h3"])
        for j, i in enumerate(tiles):
            xt = xb[:, i * D:(i + 1) * D]
            mk.op("vector", lambda e, xt=xt, i=i: e.tensor_scalar(
                hsc[:], xt, rcols[:, 16 + i:17 + i], None, ALU.mult),
                r=[tk("xb", i), rck], w=["c3_hsc"])
            for g in range(0, ND, 4):
                pb = P.psb[(g // 4) % 2]
                pbk = f"psb{(g // 4) % 2}"
                for k2 in range(4):
                    c = g + k2
                    mk.op("tensor", lambda e, c=c, k2=k2, pb=pb: e.transpose(
                        pb[:, k2 * 128:(k2 + 1) * 128],
                        hsc[:, c * 128:(c + 1) * 128], P.ident_bf[:]),
                        r=["c3_hsc", "ident_bf"], w=[pbk])
                dst = h3v[:, g:g + 4, 1 + j * 128:1 + (j + 1) * 128]
                srcv = pb[:, 0:512].rearrange("p (c t) -> p c t", c=4)
                mk.op("vector", lambda e, dst=dst, srcv=srcv: e.tensor_copy(
                    dst, srcv),
                    r=[pbk], w=["c3_h3"])
        for c in range(ND):
            cur = h3c[:, c * (TC3 + 1) + 1:c * (TC3 + 1) + 1 + TC3]
            sh = h3c[:, c * (TC3 + 1):c * (TC3 + 1) + TC3]
            mk.op("vector", lambda e, cur=cur, sh=sh: e.tensor_sub(
                asc[:], sh, cur), r=["c3_h3"], w=["c3_asc"])
            mk.op("vector", lambda e, c=c, cur=cur: e.scalar_tensor_tensor(
                xkc[:, c * TC3:(c + 1) * TC3], asc[:], maa[:, c:c + 1], cur,
                ALU.mult, ALU.add),
                r=["c3_asc", "c3_maa", "c3_h3"], w=[f"c3_xk:{c}"])
            mk.op("vector", lambda e, c=c, cur=cur: e.scalar_tensor_tensor(
                xrc[:, c * TC3:(c + 1) * TC3], asc[:], maa[:, 8 + c:9 + c], cur,
                ALU.mult, ALU.add),
                r=["c3_asc", "c3_maa", "c3_h3"], w=[f"c3_xr:{c}"])
        for tt in range(2):
            for f2 in range(2):
                ps = P.psf[4 + (tt * 2 + f2) % 2]
                pkey = f"psf{4 + (tt * 2 + f2) % 2}"
                for c in range(ND):
                    mk.op("tensor", lambda e, c=c, tt=tt, f2=f2, ps=ps: e.matmul(
                        ps[:], xrc[:, c * TC3 + tt * 128:c * TC3 + (tt + 1) * 128],
                        wrec[:, c * D + f2 * 512:c * D + f2 * 512 + 512],
                        start=(c == 0), stop=(c == ND - 1)),
                        r=[f"c3_xr:{c}", "c3_wrec"], w=[pkey])
                mk.op("scalar", lambda e, tt=tt, f2=f2, ps=ps: e.activation(
                    sigc[:, tt * D + f2 * 512:tt * D + f2 * 512 + 512], ps[:],
                    AF.Sigmoid),
                    r=[pkey], w=["c3_sig"])
        for fb in range(16):
            par = fb % 2
            wkk = f"c3_wk{par}"
            wvk = f"c3_wv{par}"
            srck = P.dram["wkey"].rearrange("p (c n) -> p c n", c=ND)[
                :, :, fb * 256:(fb + 1) * 256]
            dstk = wkb[par][:].rearrange("p (c n) -> p c n", c=ND)
            mk.dma("gpsimd", "dma_w",
                   lambda e, dstk=dstk, srck=srck: e.dma_start(out=dstk, in_=srck),
                   w=[wkk])
            srcv2 = P.dram["wval"][:, fb * 2 * D:(fb + 1) * 2 * D]
            mk.dma("gpsimd", "dma_w",
                   lambda e, par=par, srcv2=srcv2: e.dma_start(out=wvb[par][:],
                                                              in_=srcv2),
                   w=[wvk])
            for fs in range(2):
                ps = P.psf[4 + fs % 2]
                pkey = f"psf{4 + fs % 2}"
                for c in range(ND):
                    mk.op("tensor", lambda e, c=c, fs=fs, par=par, ps=ps: e.matmul(
                        ps[:, 0:TC3], wkb[par][:, c * 256 + fs * 128:
                                               c * 256 + fs * 128 + 128],
                        xkc[:, c * TC3:(c + 1) * TC3],
                        start=(c == 0), stop=(c == ND - 1)),
                        r=[f"c3_xk:{c}", wkk], w=[pkey])
                mk.op("scalar", lambda e, fs=fs, ps=ps: e.activation(
                    kfr[:, fs * TC3:(fs + 1) * TC3], ps[:, 0:TC3], AF.Relu),
                    r=[pkey], w=["c3_kfr"])
                mk.op("vector", lambda e, fs=fs: e.tensor_mul(
                    kfb[:, fs * TC3:(fs + 1) * TC3],
                    kfr[:, fs * TC3:(fs + 1) * TC3],
                    kfr[:, fs * TC3:(fs + 1) * TC3]),
                    r=["c3_kfr"], w=["c3_kfb"])
            for tt2 in range(2):
                for ch in range(2):
                    ps = P.psf[tt2 * 2 + ch]
                    pkey = f"psf{tt2 * 2 + ch}"
                    for fs in range(2):
                        mk.op("tensor", lambda e, fs=fs, tt2=tt2, ch=ch, par=par,
                              ps=ps, fb=fb: e.matmul(
                            ps[:], kfb[:, fs * TC3 + tt2 * 128:
                                       fs * TC3 + tt2 * 128 + 128],
                            wvb[par][:, fs * D + ch * 512:fs * D + ch * 512 + 512],
                            start=(fb == 0 and fs == 0),
                            stop=(fb == 15 and fs == 1)),
                            r=["c3_kfb", wvk], w=[pkey])
        for tt2 in range(2):
            i = tiles[tt2]
            for ch in range(2):
                ps = P.psf[tt2 * 2 + ch]
                pkey = f"psf{tt2 * 2 + ch}"
                mk.op("vector", lambda e, tt2=tt2, ch=ch, ps=ps: e.tensor_mul(
                    dsc[:], ps[:], sigc[:, tt2 * D + ch * 512:
                                        tt2 * D + ch * 512 + 512]),
                    r=[pkey, "c3_sig"], w=["c3_dsc"])
                xt = P.xbuf[:, i * D + ch * 512:i * D + ch * 512 + 512]
                mk.op("vector", lambda e, xt=xt, i=i: e.tensor_add(
                    xt, xt, dsc[:]),
                    r=["c3_dsc", tk("x", i)], w=[tk("x", i)])
    for nm in ("b_r3_c", "c3_wrec", "c3_maa", "c3_h3", "c3_hsc", "c3_asc",
               "c3_xk", "c3_xr", "c3_sig", "c3_kfr", "c3_kfb", "c3_dsc",
               "c3_wk0", "c3_wk1", "c3_wv0", "c3_wv1"):
        slab.free(nm)



G_SHAPES = {
    "wq": (128, ND * D), "wk": (128, ND * D), "wv": (128, ND * D),
    "wo": (128, ND * D),
    "sim": (128, ND * 16), "gates": (1, 16),
    "qall": (128, ND * D), "kall": (128, ND * D), "vall": (128, ND * D),
    "oall": (128, ND * D), "spat": (128, ND * 64), "spatT": (64, 1024),
    "ppat": (16, 1024),
    "win": (128, ND * 4384), "convw": (128, 72), "convb": (128, 18),
    "alog": (32, 1), "dm": (1, 32), "dtb": (32, 1), "mnorm": (128, 16),
    "wout": (128, 16 * 1024), "wqkv": (128, ND * 1152),
    "wcproj": (128, ND * D), "qconvw": (128, 24), "qconvb": (128, 8),
    "kvconv": (128, 4), "kkconv": (128, 4),
    "maak": (128, 8), "maar": (128, 8),
    "wkey": (128, ND * 4096), "wrec": (128, ND * D), "wval": (128, 32 * 1024),
}


def build_program(stop_after=None):
    P = Prog(stop_after)
    P.din("x", (T, D))
    for name, shape in G_SHAPES.items():
        P.din(name, shape)
    P.dout("y", (T, D))
    P.dscratch("qscr", (128, 8 * T), BF16)
    P.dscratch("xspill", (128, NT * D), F32)

    with P.nc.Block() as block:
      try:
        build_constants(P)
        load_x(P)
        lnfm = layernorm_fm(P, "x", P.xbuf, "ln1", eps=1e-5, add_back=True)
        if stop_after == "ln1":
            store_y(P, P.xbuf, "x")
            P.mk.finalize(block)
            return P
        phase_attn_a(P, lnfm)
        if stop_after == "attn_a":
            store_y(P, P.xbuf, "x")
            P.mk.finalize(block)
            return P
        phase_ffn(P, "f1")
        if stop_after == "ffn1":
            store_y(P, P.xbuf, "x")
            P.mk.finalize(block)
            return P
        phase_attnb_setup(P)
        phase_attnb_mamba(P)
        if stop_after == "attnb1":
            reload_x(P)
            for i in range(NT):
                P.mk.op("vector", lambda e, i=i: e.tensor_add(
                    P.xbuf[:, i * D:(i + 1) * D], P.xbuf[:, i * D:(i + 1) * D],
                    P.xb[:, i * D:(i + 1) * D]),
                    r=[tk("x", i), tk("xb", i)], w=[tk("x", i)])
            store_y(P, P.xbuf, "x")
            P.mk.finalize(block)
            return P
        phase_attnb_attn(P)
        if stop_after == "attnb2":
            reload_x(P)
            for i in range(NT):
                P.mk.op("vector", lambda e, i=i: e.tensor_add(
                    P.xbuf[:, i * D:(i + 1) * D], P.xbuf[:, i * D:(i + 1) * D],
                    P.xb[:, i * D:(i + 1) * D]),
                    r=[tk("x", i), tk("xb", i)], w=[tk("x", i)])
            store_y(P, P.xbuf, "x")
            P.mk.finalize(block)
            return P
        reload_x(P)
        phase_attnb_rwkv(P)
        for i in range(NT):
            P.mk.op("vector", lambda e, i=i: e.tensor_add(
                P.xbuf[:, i * D:(i + 1) * D], P.xbuf[:, i * D:(i + 1) * D],
                P.xb[:, i * D:(i + 1) * D]),
                r=[tk("x", i), tk("xb", i)], w=[tk("x", i)])
        P.slab.free("xb")
        if stop_after == "attnb3":
            store_y(P, P.xbuf, "x")
            P.mk.finalize(block)
            return P
        phase_ffn(P, "f2")
        assert set(P.slab.live) <= {"x", "ident_bf", "ident_f", "iota_sc",
                                    "ones_row_b", "ones_row_f", "ones_all_f",
                                    "ones_all_b", "cc"}, P.slab.live.keys()
        store_y(P, P.xbuf, "x")
      except StopBuild:
        pass
      P.mk.finalize(block)
    return P


_CACHE = {}


def kernel(**inputs):
    g, xs = prep_host(inputs)
    P = build_program()
    in_maps = []
    for i in range(8):
        m = {"x": xs[i]}
        m.update(g)
        in_maps.append(m)
    res = run_bass_kernel_spmd(P.nc, in_maps, core_ids=list(range(8)))
    return np.stack([r["y"] for r in res.results], axis=0)
